# version 09
# baseline (speedup 1.0000x reference)
import contextlib
import numpy as np
import ml_dtypes
import concourse.bass as bass
import concourse.mybir as mybir
from concourse.bass_utils import run_bass_kernel_spmd

F32 = mybir.dt.float32
BF16 = mybir.dt.bfloat16
ALU = mybir.AluOpType
AF = mybir.ActivationFunctionType
AXX = mybir.AxisListType.X
bfd = ml_dtypes.bfloat16

HID = 2048
KC = 16
INTER = 4096
NOC_CONV = 34
H = 64
T = 1024
EPS = 1e-5
NEXP = 16
IFF = 1408


class Ctr:
    def __init__(self, handle):
        self.h = handle
        self.n = 0


class Rec:
    ENGINES = ("sync", "tensor", "vector", "scalar", "gpsimd")

    def __init__(self):
        self.ops = {e: [] for e in self.ENGINES}

    def emit(self, eng, fn):
        self.ops[eng].append(fn)

    def replay(self, eng, h):
        for fn in self.ops[eng]:
            fn(h)


def build_program(debug=(), upto="full"):
    nc = bass.Bass("TRN2", target_bir_lowering=False)
    rec = Rec()
    st = contextlib.ExitStack()

    def din(name, shape, dt=F32):
        return nc.dram_tensor(name, shape, dt, kind="ExternalInput").ap()

    def dscr(name, shape, dt=F32):
        return nc.dram_tensor(name, shape, dt, kind="Internal").ap()

    x_d = din("x", [HID, T])
    xp_d = din("xp", [HID, T])
    sel_d = din("sel", [128, 1])
    w_in_d = din("w_in", [HID, 8576])
    b_in_d = din("b_in", [128, 67])
    conv_w_d = din("conv_w", [128, NOC_CONV * 4])
    dtb_d = din("dt_bias", [64, 1])
    alog_d = din("a_log", [64, 1])
    drow_d = din("d_row", [1, 64])
    nw_d = din("nw", [1, INTER])
    w_out_d = din("w_out", [INTER, HID])
    b_out_d = din("b_out", [128, 16])
    gate_wt_d = din("gate_wt", [HID, 16])
    w13t_d = din("w13t", [NEXP * HID, 2 * IFF])
    w2t_d = din("w2t", [NEXP * IFF, HID])
    sw13t_d = din("sw13t", [HID, 5632])
    sw2t_d = din("sw2t", [2816, HID])
    id_bf_d = din("id_bf", [128, 128], BF16)
    id_f_d = din("id_f", [128, 128])
    diag01_d = din("diag01", [128, 128], BF16)
    maskT_d = din("maskT", [128, 512])

    out_d = nc.dram_tensor("y", [HID, T], F32, kind="ExternalOutput").ap()
    dbg = {}
    for name, shape, dt in [
        ("hs", [2 * T, INTER], BF16), ("gate", [INTER, T], BF16),
        ("x1", [HID, T], F32), ("normed", [INTER, T], BF16),
        ("comb", [NEXP, T], F32), ("S", [128, INTER], F32),
        ("dtA", [128, 2 * T], F32), ("btm", [128, KC * 128], BF16),
        ("ctm", [128, KC * 128], BF16), ("ln", [128, KC * 2 * T], BF16),
        ("x2", [128, KC * T], BF16), ("yacc", [128, KC * T], F32),
    ]:
        if name in debug:
            dbg[name] = nc.dram_tensor("dbg_" + name, shape, dt,
                                       kind="ExternalOutput").ap()

    s_hs = dscr("s_hs", [2 * T, INTER], BF16)
    s_gate = dscr("s_gate", [INTER, T], BF16)
    s_normed = dscr("s_normed", [INTER, T], BF16)
    s_x1 = dscr("s_x1", [HID, T])
    s_acum = dscr("s_acum", [2, 64, 256])
    s_comb = dscr("s_comb", [NEXP, T])

    def sbt(name, shape, dt=F32):
        return st.enter_context(nc.sbuf_tensor(name, shape, dt))

    sel_v = sbt("sel_v", [128, 1])
    ones_col = sbt("ones_col", [128, 1])
    ones_row = sbt("ones_row", [1, 128])
    id_bf = sbt("id_bf_sb", [128, 128], BF16)
    id_f = sbt("id_f_sb", [128, 128])
    diag01 = sbt("diag01_sb", [128, 128], BF16)
    maskT = sbt("maskT_sb", [128, 512])
    negexpA = sbt("negexpA", [64, 1])
    dtb_s = sbt("dtb_s", [64, 1])
    b_in_s = sbt("b_in_s", [128, 67])
    conv_w_s = sbt("conv_w_s", [128, NOC_CONV * 4])
    b_out_s = sbt("b_out_s", [128, 16])
    dbc = sbt("dbc", [128, 64])
    row_s = sbt("row_s", [1, 64])
    cpool = sbt("cpool", [128, 4])

    AFW = 21504
    ABW = 54272
    AF32 = sbt("AF32", [128, AFW])
    ABF = sbt("ABF", [128, ABW], BF16)

    def _view(base, off, p, dims):
        n = int(np.prod(dims))
        a = base[0:p, off:off + n]
        if len(dims) > 1:
            names = " ".join(f"d{i}" for i in range(len(dims)))
            a = a.rearrange(f"p ({names}) -> p {names}",
                            **{f"d{i}": d for i, d in enumerate(dims)})
        return a

    def vf(off, p, *dims):
        return _view(AF32, off, p, dims)

    def vb(off, p, *dims):
        return _view(ABF, off, p, dims)

    dtA = vf(0, 128, 2 * T)
    lndt = vf(2048, 64, 2 * T)
    S_st = vf(4096, 128, INTER)
    Racum = [vf(8192, 128, 2048), vf(10240, 128, 2048)]
    p1_xt = vf(12288, 128, KC, 256)
    p1_xsq = vf(16384, 128, KC, 256)
    p1_st = vf(20480, 1, 512)
    p1_ab = vf(20992, 1, 512)
    p1_dtr = vf(16384, 64, 512)
    p1_sp1 = vf(16896, 64, 512)
    p1_sp2 = vf(17408, 64, 512)
    p2_y0 = vf(12288, 128, INTER)
    p2_expD = [vf(16384, 128, 384), vf(16768, 128, 384)]
    p2_acf = vf(17152, 64, 256)
    p2_vv = vf(17408, 64, 256)
    p2_actm = vf(17664, 128, 128)
    p2_act = vf(17664, 128, 2, 64)
    p2_eatm = vf(17792, 128, 128)
    p2_eat = vf(17792, 128, 2, 64)
    p2_csb = vf(17920, 128, 64)
    p2_ecs = vf(17984, 128, 64)
    p2_decm = vf(18048, 128, 128)
    p2_dec = vf(18048, 128, 2, 64)
    p2_wtmm = vf(18176, 128, 128)
    p2_wtm = vf(18176, 128, 2, 64)
    p2_dttm = vf(18304, 128, 128)
    p2_t1 = vf(18432, 128, 512)
    p2_t2 = vf(18944, 128, 512)
    p2_rst = vf(19456, 128, 2)
    p2_srow = vf(19460, 1, 64)
    p2_vtm = vf(19524, 128, 2, 64)
    p3_at = vf(12288, 128, 512)
    p3_xt = vf(12800, 128, 512)
    ff_y = vf(0, 128, KC, T)
    ff_lnxt = vf(0, 128, KC, 256)
    ff_lnsq = vf(4096, 128, KC, 256)
    ff_lnst = vf(8192, 1, 512)
    ff_lnab = vf(8704, 1, 512)
    ff_cbc = vf(16384, 128, T)
    ff_sgc = vf(17408, 128, 512)
    ff_sgc2 = vf(20032, 128, 512)
    ff_sc = vf(17920, 128, 16)
    ff_sc2 = vf(17936, 128, 16)
    ff_m = vf(17952, 128, 1)
    ff_den = vf(17956, 128, 1)
    ff_cmb = vf(17984, 16, 1024)
    ff_xo = vf(19008, 128, 1024)

    B_tm = vb(0, 128, KC, 128)
    C_tm = vb(2048, 128, KC, 128)
    S_bf = vb(4096, 128, INTER)
    p1_ln = vb(8192, 128, KC, 2 * T)
    p1_ci = [vb(40960, 128, 2056), vb(43016, 128, 2056)]
    p1_dg = [vb(45072, 128, 4, 128), vb(45584, 128, 4, 128)]
    p1_w = [vb(46096, 128, KC, 128), vb(48144, 128, KC, 128)]
    p1_stg = vb(50192, 128, 16, 128)
    p1_stgf = vb(50192, 128, 2048)
    p1_gstg = [vb(40960, 128, 1024), vb(43016, 128, 1024)]
    p2_hsb = [vb(8192, 128, 2, INTER), vb(40704, 128, 2, INTER)]
    p2_gf = vb(16384, 128, 32, 256)
    p2_gt = vb(24576, 128, 2, INTER)
    p2_gm = vb(32768, 128, 512)
    p2_gm2 = vb(48896, 128, 384)
    p2_mt = [vb(33280, 128, 384), vb(33664, 128, 384)]
    p2_nw = vb(34048, 128, INTER)
    p2_u = [vb(38144, 128, 512), vb(38656, 128, 512)]
    p2_nst = [vb(39168, 128, 512), vb(39680, 128, 512)]
    p2_bfm = vb(40192, 128, 256)
    p2_cfm = vb(40448, 128, 256)
    p2_y1 = ABF[0:128, 16384:16384 + 8192].bitcast(F32)
    p3_nm = vb(0, 128, 32, T)
    p3_w = [vb(32768, 128, 32, 128), vb(36864, 128, 32, 128)]
    ff_x2 = vb(0, 128, KC, T)
    ff_act = vb(16384, 128, 22, T)
    ff_w13 = [vb(38912, 128, KC, 256), vb(43008, 128, KC, 256)]
    ff_w2 = [vb(47104, 128, 22, 128), vb(49920, 128, 22, 128)]
    ff_gw = vb(52736, 128, 16, 16)

    pb = [st.enter_context(nc.psum_tensor(f"pb{i}", [128, 512], F32))
          for i in range(8)]

    c_sy = Ctr(st.enter_context(nc.semaphore("c_sy")))
    c_gp = Ctr(st.enter_context(nc.semaphore("c_gp")))
    c_pe = Ctr(st.enter_context(nc.semaphore("c_pe")))
    c_ve = Ctr(st.enter_context(nc.semaphore("c_ve")))
    c_ac = Ctr(st.enter_context(nc.semaphore("c_ac")))
    ALLC = {"sync": c_sy, "gpsimd": c_gp, "tensor": c_pe,
            "vector": c_ve, "scalar": c_ac}

    def op(eng, method, *args, waits=(), inc=None, k=1, **kw):
        if inc is not None:
            inc.n += k
        wlist = tuple((c.h, v) for c, v in waits if v > 0)

        def fn(h, method=method, args=args, kw=kw, wlist=wlist, inc=inc, k=k):
            for hh, v in wlist:
                h.wait_ge(hh, v)
            instr = getattr(h, method)(*args, **kw)
            if inc is not None:
                instr.then_inc(inc.h, k)
        rec.emit(eng, fn)

    def w(*ctrs):
        return [(c, c.n) for c in ctrs]

    def barrier():
        snap = [(c, c.n) for c in ALLC.values()]
        for eng, c in ALLC.items():
            op(eng, "nop", waits=[(cc, v) for cc, v in snap if cc is not c],
               inc=c)

    def dma(eng, out, in_, waits=()):
        ctr = c_sy if eng == "sync" else c_gp
        op(eng, "dma_start", waits=waits, inc=ctr, k=16, out=out, in_=in_)

    def mm(out, lhsT, rhs, start=True, stop=True, waits=()):
        op("tensor", "matmul", out, lhsT, rhs, start=start, stop=stop,
           waits=waits, inc=(c_pe if stop else None))

    def tt(out, a, b, alu, waits=(), eng="vector"):
        op(eng, "tensor_tensor", out, a, b, alu, waits=waits,
           inc=(c_ve if eng == "vector" else c_gp))

    def ts(out, a, s1, alu, waits=(), eng="vector"):
        op(eng, "tensor_scalar", out, a, s1, None, alu, waits=waits,
           inc=(c_ve if eng == "vector" else c_gp))

    def act(out, in_, func, waits=(), **kw):
        op("scalar", "activation", out, in_, func, waits=waits, inc=c_ac, **kw)

    def cp(out, in_, waits=(), eng="vector"):
        op(eng, "tensor_copy", out, in_, waits=waits,
           inc=(c_ve if eng == "vector" else c_gp))

    def red(out, in_, alu, waits=()):
        op("vector", "tensor_reduce", out, in_, AXX, alu, waits=waits, inc=c_ve)

    def bc8(apview, h0):
        return apview[:, h0:h0 + 8].unsqueeze(2).broadcast_to((128, 8, 64))

    def flat(v):
        return v.rearrange("p a b -> p (a b)")

    dma("sync", sel_v[:], sel_d[:])
    dma("sync", id_bf[:], id_bf_d[:])
    dma("sync", id_f[:], id_f_d[:])
    dma("sync", diag01[:], diag01_d[:])
    dma("sync", maskT[:], maskT_d[:])
    dma("sync", dtb_s[:], dtb_d[:])
    dma("sync", b_in_s[:], b_in_d[:])
    dma("sync", conv_w_s[:], conv_w_d[:])
    dma("sync", b_out_s[:], b_out_d[:])
    dma("sync", row_s[:], drow_d[:])
    dma("sync", negexpA[:], alog_d[:])
    op("vector", "memset", ones_col[:], 1.0, inc=c_ve)
    op("vector", "memset", ones_row[:], 1.0, inc=c_ve)
    op("vector", "memset", S_st[:], 0.0, inc=c_ve)
    op("vector", "memset", S_bf[:], 0.0, inc=c_ve)
    act(negexpA[:], negexpA[:], AF.Exp, waits=w(c_sy))
    ts(negexpA[:], negexpA[:], -1.0, ALU.mult, waits=w(c_ac))
    mm(pb[0][:, 0:64], ones_row[:], row_s[:], waits=w(c_sy, c_ve))
    cp(dbc[:], pb[0][:, 0:64], waits=w(c_pe))
    for ci, cv in enumerate((EPS, 1.0, 1e-30)):
        op("vector", "memset", cpool[:, ci:ci + 1], cv, inc=c_ve)
        nc.const_aps.register(F32, cv, cpool[:, ci:ci + 1])

    def ln_quarter(src_slice, dst_fn, xt, xsq, strow, abrow):
        dma("sync", xt[:],
            src_slice.rearrange("(kc p) t -> p kc t", p=128),
            waits=w(c_ve, c_ac, c_pe))
        ld = w(c_sy)
        act(flat(xsq), flat(xt), AF.Square, waits=ld)
        for kc in range(KC):
            mm(pb[7][0:1, 0:256], ones_col[:], xt[:, kc, :],
               start=(kc == 0), stop=(kc == KC - 1),
               waits=(ld + w(c_ve) if kc == 0 else ()))
        for kc in range(KC):
            mm(pb[7][0:1, 256:512], ones_col[:], xsq[:, kc, :],
               start=(kc == 0), stop=(kc == KC - 1),
               waits=(w(c_ac) if kc == 0 else ()))
        ts(strow[:], pb[7][0:1, :], 1.0 / HID, ALU.mult, waits=w(c_pe))
        mrow, e2row = strow[0:1, 0:256], strow[0:1, 256:512]
        arow, brow = abrow[0:1, 0:256], abrow[0:1, 256:512]
        tt(brow, mrow, mrow, ALU.mult, waits=w(c_ve))
        tt(brow, e2row, brow, ALU.subtract, waits=w(c_ve))
        act(brow, brow, AF.Ln, bias=EPS, waits=w(c_ve))
        act(arow, brow, AF.Exp, scale=-0.5)
        tt(brow, mrow, arow, ALU.mult, waits=w(c_ac))
        ts(brow, brow, -1.0, ALU.mult, waits=w(c_ve))
        mm(pb[6][:, :], ones_row[:], abrow[0:1, :], waits=w(c_ve))
        a_bc = pb[6][:, 0:256].unsqueeze(1).broadcast_to((128, KC, 256))
        b_bc = pb[6][:, 256:512].unsqueeze(1).broadcast_to((128, KC, 256))
        tt(xt[:, :, :], xt[:, :, :], a_bc, ALU.mult, waits=w(c_pe))
        tt(dst_fn(None), xt[:, :, :], b_bc, ALU.add)

    for qg in range(8):
        sti, q = qg // 4, qg % 4
        src = (xp_d if sti == 0 else x_d)[:, q * 256:(q + 1) * 256]
        off = sti * T + q * 256
        ln_quarter(src, lambda kc, off=off: p1_ln[:, :, off:off + 256],
                   p1_xt, p1_xsq, p1_st, p1_ab)
    barrier()
    if "ln" in dbg:
        dma("sync", dbg["ln"][:], flat(p1_ln))
        barrier()

    w_used = [0, 0]

    def load_wtile(buf, col0, par):
        dma("gpsimd", buf[:, :, :],
            w_in_d[:, col0:col0 + 128].rearrange("(kc p) c -> p kc c", p=128),
            waits=[(c_pe, w_used[par])])

    pfree = {}

    def proj_sweep(wbuf, n_m, banks, evac_fn, sti_list=(0, 1)):
        gi = 0
        for sti in sti_list:
            for th in range(2):
                bank = banks[gi % len(banks)]
                ps = bank[0:n_m, :]
                for kc in range(KC):
                    mm(ps, wbuf[:, kc, 0:n_m],
                       p1_ln[:, kc, sti * T + th * 512: sti * T + th * 512 + 512],
                       start=(kc == 0), stop=(kc == KC - 1),
                       waits=((w(c_gp) if gi == 0 else []) +
                              [pfree.get(id(bank), (c_ve, 0))]
                              if kc == 0 else ()))
                pfree[id(bank)] = evac_fn(ps, sti, th)
                gi += 1

    for oc in range(NOC_CONV):
        par = oc % 2
        ci = p1_ci[par]
        dg = p1_dg[par]
        load_wtile(p1_w[par], INTER + oc * 128, par)

        def evac(ps, sti, th, ci=ci, oc=oc):
            dst = ci[:, 3 + sti * 1027 + th * 512: 3 + sti * 1027 + th * 512 + 512]
            act(dst, ps, AF.Identity, bias=b_in_s[:, 32 + oc:33 + oc], waits=w(c_pe))
            return (c_ac, c_ac.n)

        proj_sweep(p1_w[par], 128, [pb[par], pb[4 + par]], evac)
        w_used[par] = c_pe.n
        op("vector", "memset", ci[:, 0:3], 0.0, waits=w(c_ac), inc=c_ve)
        ts(ci[:, 1027:1030], ci[:, 1024:1027], sel_v[:], ALU.mult)
        for k in range(4):
            ts(dg[:, k, :], diag01[:], conv_w_s[:, 4 * oc + k:4 * oc + k + 1],
               ALU.mult)
        dvr = w(c_ve)
        for tbq in range(4):
            bank = pb[2 + (tbq % 2)]
            ps = bank[:, :]
            for tb4 in range(4):
                tb = tbq * 4 + tb4
                sti, tbl = tb // 8, tb % 8
                base = sti * 1027 + tbl * 128
                for k in range(4):
                    mm(ps[:, tb4 * 128:tb4 * 128 + 128],
                       ci[:, base + k: base + k + 128], dg[:, k, :],
                       start=(k == 0), stop=(k == 3),
                       waits=((dvr if tb4 == 0 and k == 0 else []) +
                              ([pfree.get(id(bank), (c_ve, 0))]
                               if (tb4 == 0 and k == 0) else [])))
            if oc < 32:
                act(p1_stgf[:, tbq * 512:(tbq + 1) * 512], ps[:, :], AF.Silu,
                    waits=w(c_pe) + (w(c_sy) if tbq == 0 else []))
            else:
                dstp = B_tm if oc == 32 else C_tm
                for tb4 in range(4):
                    tb = tbq * 4 + tb4
                    act(dstp[:, tb, :], ps[:, tb4 * 128:tb4 * 128 + 128],
                        AF.Silu, waits=(w(c_pe) if tb4 == 0 else ()))
            pfree[id(bank)] = (c_ac, c_ac.n)
        if oc < 32:
            dma("sync", s_hs.rearrange("(tb p) c -> p tb c", p=128)
                [:, :, oc * 128:oc * 128 + 128], p1_stg[:], waits=w(c_ac))
        w_used[par] = c_pe.n

    load_wtile(p1_w[0], 8448, 0)

    def dt_evac(ps, sti, th):
        col = sti * T + th * 512
        act(p1_dtr[:], ps, AF.Identity, bias=b_in_s[0:64, 66:67], waits=w(c_pe))
        dt_evac.tick = c_ac.n
        ts(p1_dtr[:], p1_dtr[:], dtb_s[:, 0:1], ALU.add, waits=w(c_ac))
        act(p1_sp1[:], p1_dtr[:], AF.Abs, waits=w(c_ve))
        act(p1_sp1[:], p1_sp1[:], AF.Exp, scale=-1.0)
        act(p1_sp1[:], p1_sp1[:], AF.Ln, bias=1.0)
        ts(p1_sp2[:], p1_dtr[:], 0.0, ALU.max, waits=w(c_ac))
        tt(dtA[0:64, col:col + 512], p1_sp2[:], p1_sp1[:], ALU.add)
        ts(dtA[64:128, col:col + 512], dtA[0:64, col:col + 512],
           negexpA[:], ALU.mult)
        act(lndt[:, col:col + 512], dtA[0:64, col:col + 512], AF.Ln,
            bias=1e-30, waits=w(c_ve))
        return (c_ac, dt_evac.tick)

    proj_sweep(p1_w[0], 64, [pb[4]], dt_evac)
    w_used[0] = c_pe.n
    p1_ac_tick = c_ac.n
    p1_sy_tick = c_sy.n

    for oc in range(32):
        par = oc % 2
        load_wtile(p1_w[par], oc * 128, par)
        stg = p1_gstg[par]

        def gevac(ps, sti, th, stg=stg, oc=oc):
            act(stg[:, th * 512:(th + 1) * 512], ps, AF.Identity,
                bias=b_in_s[:, oc:oc + 1], waits=w(c_pe) + w(c_sy))
            return (c_ac, c_ac.n)

        proj_sweep(p1_w[par], 128, [pb[par], pb[4 + par]], gevac, sti_list=(1,))
        dma("sync", s_gate[oc * 128:(oc + 1) * 128, :], stg[:], waits=w(c_ac))
        w_used[par] = c_pe.n


    barrier()
    for nm, src in [("hs", s_hs), ("gate", s_gate)]:
        if nm in dbg:
            dma("sync", dbg[nm][:], src[:])
    if "dtA" in dbg:
        dma("sync", dbg["dtA"][:], dtA)
    if "btm" in dbg:
        dma("sync", dbg["btm"][:], flat(B_tm))
    if "ctm" in dbg:
        dma("sync", dbg["ctm"][:], flat(C_tm))
    barrier()
    if upto == "p1":
        op("gpsimd", "nop", waits=w(c_sy))
        return _finish(nc, rec, st, dbg)

    dma("gpsimd", p2_nw[:], nw_d[:].partition_broadcast(128).squeeze(1))

    hs_used = {0: (c_pe, 0), 1: (c_pe, 0)}
    hs_used_ve = {}
    hs_used_ac = {}

    def chunk_prelude(cidx, own, extra=()):
        par = cidx % 2
        col = cidx * 256
        rows = cidx * 256
        p2_hs = p2_hsb[par]
        dma("sync", p2_hs[:, 0, :], s_hs[rows:rows + 128, :],
            waits=[hs_used[par], (c_ve, hs_used_ve.get(par, 0)),
                   (c_ac, hs_used_ac.get(par, 0))] + list(extra))
        dma("sync", p2_hs[:, 1, :], s_hs[rows + 128:rows + 256, :])
        if own:
            dma("sync", p2_gf[:, :, :],
                s_gate.rearrange("(cc p) t -> p cc t", p=128)
                [:, :, col - T:col - T + 256],
                waits=w(c_pe, c_ac, c_ve))
        op("vector", "tensor_tensor_scan", p2_acf[:], dtA[64:128, col:col + 256],
           dtA[64:128, col:col + 256], 0.0, ALU.add, ALU.bypass,
           waits=w(c_pe, c_sy), inc=c_ve)
        if own:
            tt(p2_vv[:], lndt[:, col:col + 256], p2_acf[:], ALU.subtract,
               waits=w(c_ve))
            dma("sync", s_acum[par], p2_acf[:], waits=w(c_ve))
            chunk_prelude.acum_tick = c_sy.n
        for lh in range(2):
            mm(pb[2][:, lh * 64:lh * 64 + 64],
               p2_acf[:, lh * 128:lh * 128 + 128], id_f[0:64, 0:64],
               waits=(w(c_ve) if lh == 0 else ()))
            mm(pb[2][:, 128 + lh * 64:128 + lh * 64 + 64],
               dtA[0:64, col + lh * 128:col + lh * 128 + 128], id_f[0:64, 0:64])
        mm(pb[2][0:1, 256:320], p2_acf[:, 255:256], id_f[0:64, 0:64])
        if own:
            for lh in range(2):
                mm(pb[2][:, 384 + lh * 64:384 + lh * 64 + 64],
                   p2_vv[:, lh * 128:lh * 128 + 128], id_f[0:64, 0:64])
        cp(p2_actm[:], pb[2][:, 0:128], waits=w(c_pe))
        cp(p2_dttm[:], pb[2][:, 128:256])
        cp(p2_srow[:], pb[2][0:1, 256:320])
        if own:
            cp(p2_vtm.rearrange("p a b -> p (a b)"), pb[2][:, 384:512])
        mm(pb[2][:, 320:384], ones_row[:], p2_srow[:], waits=w(c_ve))
        cp(p2_csb[:], pb[2][:, 320:384], waits=w(c_pe))
        if own:
            act(p2_eatm[:], p2_actm[:], AF.Exp, waits=w(c_ve))
        act(p2_ecs[:], p2_csb[:], AF.Exp, waits=w(c_ve))
        for lh in range(2):
            tt(p2_dec[:, lh, :], p2_csb[:], p2_act[:, lh, :], ALU.subtract,
               waits=(w(c_ac) if lh == 0 else ()))
        act(p2_decm[:], p2_decm[:], AF.Exp, waits=w(c_ve))
        tt(p2_wtmm[:], p2_decm[:], p2_dttm[:], ALU.mult, waits=w(c_ac))

    def chunk_states(cidx):
        tb0, tb1 = cidx * 2, cidx * 2 + 1
        par = cidx % 2
        p2_hs = p2_hsb[par]
        for lh in range(2):
            for sl in range(8):
                tt(p2_hs[:, lh, sl * 512:(sl + 1) * 512],
                   p2_hs[:, lh, sl * 512:(sl + 1) * 512],
                   bc8(p2_wtm[:, lh, :], sl * 8), ALU.mult,
                   waits=(w(c_pe, c_gp) if (lh == 0 and sl == 0) else ()))
        for sl in range(8):
            ps = pb[7][:, :]
            mm(ps, B_tm[:, tb0, :], p2_hs[:, 0, sl * 512:(sl + 1) * 512],
               start=True, stop=False, waits=w(c_ve))
            mm(ps, B_tm[:, tb1, :], p2_hs[:, 1, sl * 512:(sl + 1) * 512],
               start=False, stop=True)
            tt(S_st[:, sl * 512:(sl + 1) * 512],
               S_st[:, sl * 512:(sl + 1) * 512], bc8(p2_ecs, sl * 8), ALU.mult,
               waits=w(c_pe))
            tt(S_st[:, sl * 512:(sl + 1) * 512],
               S_st[:, sl * 512:(sl + 1) * 512], ps[:, :], ALU.add)
        cp(S_bf[:], S_st[:])
        hs_used[par] = (c_pe, c_pe.n)
        hs_used_ve[par] = c_ve.n

    for cidx in range(4):
        chunk_prelude(cidx, own=False,
                      extra=([(c_ac, p1_ac_tick), (c_sy, p1_sy_tick)]
                             if cidx == 0 else ()))
        chunk_states(cidx)
    ts(S_st[:], S_st[:], sel_v[:], ALU.mult)
    cp(S_bf[:], S_st[:])
    if "S" in dbg:
        barrier()
        dma("sync", dbg["S"][:], S_st[:], waits=w(c_ve))
        barrier()

    barrier()

    for cidx in range(4, 8):
        par = cidx % 2
        ocol = cidx * 256 - T
        p2_hs = p2_hsb[par]
        chunk_prelude(cidx, own=True)
        for lh in range(2):
            for ccq in range(8):
                ps = pb[3 + (ccq % 2)]
                for c4 in range(4):
                    cc = ccq * 4 + c4
                    mm(ps[:, c4 * 128:c4 * 128 + 128],
                       p2_gf[:, cc, lh * 128:lh * 128 + 128], id_bf[:],
                       waits=(w(c_sy, c_ve) if c4 == 0 else ()))
                cp(p2_gt[:, lh, ccq * 512:(ccq + 1) * 512], ps[:, :],
                   waits=w(c_pe))
        mm(pb[2][:, 0:128], B_tm[:, cidx * 2, :], id_bf[:], waits=w(c_ve))
        mm(pb[2][:, 128:256], B_tm[:, cidx * 2 + 1, :], id_bf[:])
        mm(pb[2][:, 256:384], C_tm[:, cidx * 2, :], id_bf[:])
        mm(pb[2][:, 384:512], C_tm[:, cidx * 2 + 1, :], id_bf[:])
        cp(p2_bfm[:], pb[2][:, 0:256], waits=w(c_pe))
        cp(p2_cfm[:], pb[2][:, 256:512])
        for sh in range(2):
            mm(pb[2][:, sh * 256:sh * 256 + 256],
               p2_bfm[:, sh * 128:sh * 128 + 128], p2_cfm[:],
               waits=(w(c_ve) if sh == 0 else ()))
        tt(p2_gm[:], pb[2][:, :], maskT[:], ALU.mult, waits=w(c_pe))
        cp(p2_gm2[:, 0:256], p2_gm[:, 0:256], waits=w(c_ve))
        cp(p2_gm2[:, 256:384], p2_gm[:, 384:512])
        gmr = w(c_ve)

        rc_ready = {}
        rc_used = [0, 0]
        expD_free = [(c_ac, 0), (c_ac, 0)]
        mt_free = [(c_pe, 0), (c_pe, 0)]
        ybank_free = (c_ve, 0)
        for g8 in range(8):
            rpar = g8 % 2
            h0 = g8 * 8
            dma("sync", Racum[rpar][:],
                s_acum[par, h0:h0 + 8, :].rearrange("h l -> (h l)")
                .partition_broadcast(128),
                waits=[(c_ve, rc_used[rpar]),
                       (c_sy, chunk_prelude.acum_tick)])
            rc_ready[g8] = c_sy.n
            for hh in range(8):
                h = h0 + hh
                dpar = h % 2
                gidx = h % 8
                op("vector", "tensor_scalar", p2_expD[dpar][:, 0:256],
                   Racum[rpar][:, hh * 256:hh * 256 + 256],
                   p2_vtm[:, 0, h:h + 1], 30.0, ALU.add, ALU.min,
                   waits=([(c_sy, rc_ready[g8])] if hh == 0 else []) +
                   [expD_free[dpar]], inc=c_ve)
                op("vector", "tensor_scalar", p2_expD[dpar][:, 256:384],
                   Racum[rpar][:, hh * 256 + 128:hh * 256 + 256],
                   p2_vtm[:, 1, h:h + 1], 30.0, ALU.add, ALU.min, inc=c_ve)
                rc_used[rpar] = c_ve.n
                act(p2_mt[dpar][:], p2_expD[dpar][:], AF.Exp,
                    waits=w(c_ve) + [mt_free[dpar]])
                expD_free[dpar] = (c_ac, c_ac.n)
                tt(p2_mt[dpar][:], p2_mt[dpar][:], p2_gm2[:], ALU.mult,
                   waits=w(c_ac))
                mtw = w(c_ve)
                mm(pb[3][:, gidx * 64:gidx * 64 + 64], p2_mt[dpar][:, 0:128],
                   p2_hs[:, 0, h * 64:h * 64 + 64],
                   waits=mtw + ([ybank_free] if gidx == 0 else []))
                mm(pb[4][:, gidx * 64:gidx * 64 + 64], p2_mt[dpar][:, 128:256],
                   p2_hs[:, 0, h * 64:h * 64 + 64], start=True, stop=False)
                mm(pb[4][:, gidx * 64:gidx * 64 + 64], p2_mt[dpar][:, 256:384],
                   p2_hs[:, 1, h * 64:h * 64 + 64], start=False, stop=True)
                mm(pb[5][:, gidx * 64:gidx * 64 + 64], p2_cfm[:, 0:128],
                   S_bf[:, h * 64:h * 64 + 64],
                   waits=(gmr if gidx == 0 else ()))
                mm(pb[6][:, gidx * 64:gidx * 64 + 64], p2_cfm[:, 128:256],
                   S_bf[:, h * 64:h * 64 + 64])
                mt_free[dpar] = (c_pe, c_pe.n)
            if True:
                gg = g8
                h0g = gg * 8
                for lh, ydb, yob, ybuf in ((0, 3, 5, p2_y0), (1, 4, 6, p2_y1)):
                    tt(p2_u[lh], p2_hs[:, lh, h0g * 64:h0g * 64 + 512],
                       bc8(dbc, h0g), ALU.mult, eng="gpsimd", waits=w(c_pe))
                    tt(p2_t1[:], pb[yob][:, :], bc8(p2_eat[:, lh, :], h0g),
                       ALU.mult, waits=w(c_pe, c_gp))
                    tt(p2_t2[:], p2_t1[:], pb[ydb][:, :], ALU.add)
                    tt(ybuf[:, h0g * 64:h0g * 64 + 512], p2_t2[:], p2_u[lh],
                       ALU.add)
                ybank_free = (c_ve, c_ve.n)
        yev = w(c_ve)
        chunk_states(cidx)
        for lh, ybuf in ((0, p2_y0), (1, p2_y1)):
            gt = p2_gt[:, lh, :]
            act(gt, gt, AF.Silu, waits=yev)
            tt(ybuf[:], ybuf[:], gt, ALU.mult, waits=w(c_ac))
            act(p2_hs[:, 0, :], ybuf[:], AF.Square,
                accum_out=p2_rst[:, lh:lh + 1],
                waits=w(c_ve, c_sy) + [hs_used[par]])
            ts(p2_rst[:, lh:lh + 1], p2_rst[:, lh:lh + 1], 1.0 / INTER,
               ALU.mult, waits=w(c_ac))
            act(p2_rst[:, lh:lh + 1], p2_rst[:, lh:lh + 1], AF.Ln, bias=EPS,
                waits=w(c_ve))
            act(p2_rst[:, lh:lh + 1], p2_rst[:, lh:lh + 1], AF.Exp, scale=-0.5)
            op("vector", "scalar_tensor_tensor", gt, ybuf[:],
               p2_rst[:, lh:lh + 1], p2_nw[:], ALU.mult, ALU.mult,
               waits=w(c_ac, c_gp), inc=c_ve)
            for ccq in range(8):
                ps = pb[3 + (ccq % 2)]
                nst = p2_nst[ccq % 2]
                for c4 in range(4):
                    cc = ccq * 4 + c4
                    mm(ps[:, c4 * 128:c4 * 128 + 128],
                       p2_gt[:, lh, cc * 128:cc * 128 + 128], id_bf[:],
                       waits=(w(c_ve) + [(c_sy, c_sy.n)] if c4 == 0 else ()))
                cp(nst[:], ps[:, :], waits=w(c_pe))
                dma("sync", s_normed.rearrange("(cc p) t -> p cc t", p=128)
                    [:, ccq * 4:(ccq + 1) * 4,
                     ocol + lh * 128: ocol + lh * 128 + 128],
                    nst[:].rearrange("p (a b) -> p a b", a=4), waits=w(c_ve))
        hs_used_ac[par] = c_ac.n

    barrier()
    if "normed" in dbg:
        dma("sync", dbg["normed"][:], s_normed[:])
        barrier()
    if upto == "p2":
        op("gpsimd", "nop", waits=w(c_sy))
        return _finish(nc, rec, st, dbg)

    dma("sync", p3_nm[:, :, :],
        s_normed.rearrange("(kc p) t -> p kc t", p=128))
    wo_used = [0, 0]
    p3free = {}
    p3at = [p3_at, p3_xt]
    p3xt = [vf(13312, 128, 512), vf(13824, 128, 512)]
    for oc in range(KC):
        par = oc % 2
        wbuf = p3_w[par]
        dma("gpsimd", wbuf[:, :, :],
            w_out_d[:, oc * 128:(oc + 1) * 128]
            .rearrange("(kc p) c -> p kc c", p=128),
            waits=[(c_pe, wo_used[par])])
        wld0 = w(c_gp) + ([(c_sy, c_sy.n)] if oc == 0 else [])
        for th in range(2):
            bi = (oc * 2 + th) % 4
            bank = pb[bi]
            ps = bank[:, :]
            for kc in range(32):
                mm(ps, wbuf[:, kc, :], p3_nm[:, kc, th * 512:(th + 1) * 512],
                   start=(kc == 0), stop=(kc == 31),
                   waits=((wld0 if th == 0 else []) +
                          [p3free.get(id(bank), (c_ve, 0))]
                          if kc == 0 else ()))
            ab = p3at[th]
            xb = p3xt[th]
            act(ab[:], ps, AF.Identity, bias=b_out_s[:, oc:oc + 1],
                waits=w(c_pe, c_ve))
            p3free[id(bank)] = (c_ac, c_ac.n)
            dma("sync", xb[:],
                x_d[oc * 128:(oc + 1) * 128, th * 512:(th + 1) * 512],
                waits=w(c_ve))
            tt(xb[:], xb[:], ab[:], ALU.add, waits=w(c_sy, c_ac))
            dma("sync", s_x1[oc * 128:(oc + 1) * 128, th * 512:(th + 1) * 512],
                xb[:], waits=w(c_ve))
        wo_used[par] = c_pe.n
    barrier()
    if "x1" in dbg:
        dma("sync", dbg["x1"][:], s_x1[:])
        barrier()
    if upto == "p3":
        op("gpsimd", "nop", waits=w(c_sy))
        return _finish(nc, rec, st, dbg)

    for q in range(4):
        src = s_x1[:, q * 256:(q + 1) * 256]
        off = q * 256
        ln_quarter(src, lambda kc, off=off: ff_x2[:, :, off:off + 256],
                   ff_lnxt, ff_lnsq, ff_lnst, ff_lnab)
    barrier()
    if "x2" in dbg:
        dma("sync", dbg["x2"][:], flat(ff_x2))
        barrier()

    dma("gpsimd", ff_gw[:, :, :],
        gate_wt_d[:].rearrange("(kc p) e -> p kc e", p=128))
    sc_all = ff_cbc[:, 0:128].rearrange("p (a b) -> p a b", a=8)
    sc2_all = ff_cbc[:, 128:256].rearrange("p (a b) -> p a b", a=8)
    ge_all = ff_cbc[:, 256:384].rearrange("p (a b) -> p a b", a=8)
    m_all = ff_cbc[:, 384:392].unsqueeze(2)
    den_all = ff_cbc[:, 392:400].unsqueeze(2)
    for tb in range(8):
        ps = pb[0]
        for kc in range(KC):
            mm(ps[0:128, tb * 16:tb * 16 + 16],
               ff_x2[:, kc, tb * 128:(tb + 1) * 128],
               ff_gw[:, kc, :], start=(kc == 0), stop=(kc == KC - 1),
               waits=(w(c_gp, c_ve) if (tb == 0 and kc == 0) else ()))
    cp(sc_all, pb[0][:, 0:128].rearrange("p (a b) -> p a b", a=8),
       waits=w(c_pe))
    def mbc():
        return m_all.broadcast_to((128, 8, 16))
    red(m_all.squeeze(2), sc_all.rearrange("p a b -> p (a b)") if False else m_all.squeeze(2), ALU.max) if False else None
    op("vector", "tensor_reduce", m_all, sc_all, AXX, ALU.max,
       waits=w(c_ve), inc=c_ve)
    tt(sc_all, sc_all, mbc(), ALU.subtract, waits=w(c_ve))
    act(ff_cbc[:, 0:128], ff_cbc[:, 0:128], AF.Exp, waits=w(c_ve))
    cp(sc2_all, sc_all, waits=w(c_ac))
    for r in range(5):
        op("vector", "tensor_reduce", m_all, sc2_all, AXX, ALU.max,
           waits=w(c_ve), inc=c_ve)
        tt(ge_all, sc2_all, mbc(), ALU.is_ge, waits=w(c_ve))
        op("vector", "scalar_tensor_tensor",
           ff_cbc[:, 128:256], ff_cbc[:, 256:384], -4.0,
           ff_cbc[:, 128:256], ALU.mult, ALU.add, waits=w(c_ve), inc=c_ve)
    op("vector", "tensor_reduce", m_all, sc2_all, AXX, ALU.max,
       waits=w(c_ve), inc=c_ve)
    tt(ge_all, sc_all, mbc(), ALU.is_ge, waits=w(c_ve))
    tt(ff_cbc[:, 0:128], ff_cbc[:, 0:128], ff_cbc[:, 256:384], ALU.mult,
       waits=w(c_ve))
    op("vector", "tensor_reduce", den_all, sc_all, AXX, ALU.add,
       waits=w(c_ve), inc=c_ve)
    ts(ff_cbc[:, 392:400], ff_cbc[:, 392:400], 1e-20, ALU.add, waits=w(c_ve))
    op("vector", "reciprocal", ff_cbc[:, 392:400], ff_cbc[:, 392:400],
       waits=w(c_ve), inc=c_ve)
    tt(sc_all, sc_all, den_all.broadcast_to((128, 8, 16)), ALU.mult,
       waits=w(c_ve))
    for tb in range(8):
        psc = pb[1] if tb < 4 else pb[2]
        mm(psc[0:16, (tb % 4) * 128:(tb % 4) * 128 + 128],
           ff_cbc[:, tb * 16:tb * 16 + 16], id_f[:], waits=w(c_ve))
        cp(ff_cmb[:, tb * 128:(tb + 1) * 128],
           psc[0:16, (tb % 4) * 128:(tb % 4) * 128 + 128], waits=w(c_pe))
    dma("sync", s_comb[:], ff_cmb[:], waits=w(c_ve))
    if "comb" in dbg:
        dma("sync", dbg["comb"][:], s_comb[:], waits=w(c_sy))
    op("vector", "memset", flat(ff_y), 0.0, waits=w(c_sy, c_pe, c_ac),
       inc=c_ve)
    barrier()
    if upto == "gate":
        op("gpsimd", "nop", waits=w(c_sy))
        return _finish(nc, rec, st, dbg)

    state = {"w13": [0, 0], "w2": [0, 0], "cbc": 0,
             "bfree": [(c_ve, 0)] * 8, "seq": 0,
             "sgfree": [(c_ve, 0), (c_ve, 0)],
             "yfree": [(c_ve, 0), (c_ve, 0)]}

    def expert(eidx):
        shared = (eidx == NEXP)
        nih = 22 if shared else 11
        if not shared:
            dma("sync", ff_cbc[:],
                s_comb[eidx:eidx + 1, :].partition_broadcast(128).squeeze(1),
                waits=[(c_ve, state["cbc"])])
        cbw = c_sy.n
        for icp in range(nih):
            ipar = icp % 2
            if shared:
                srcw = sw13t_d[:, icp * 256:(icp + 1) * 256]
            else:
                srcw = w13t_d[eidx * HID:(eidx + 1) * HID,
                              icp * 256:(icp + 1) * 256]
            dma("gpsimd", ff_w13[ipar][:, :, :],
                srcw.rearrange("(kc p) c -> p kc c", p=128),
                waits=[(c_pe, state["w13"][ipar])])
            wld = w(c_gp)
            wbuf = ff_w13[ipar]
            GB = (0, 1, 2)
            UB = (3, 6, 7)
            for th in range(2):
                seq = state["seq"]
                gb, ub = GB[seq % 3], UB[seq % 3]
                for gu in range(2):
                    bk = gb if gu == 0 else ub
                    ps = pb[bk][:, :]
                    fr = state["bfree"][bk]
                    for kc in range(KC):
                        mm(ps, wbuf[:, kc, gu * 128:gu * 128 + 128],
                           ff_x2[:, kc, th * 512:(th + 1) * 512],
                           start=(kc == 0), stop=(kc == KC - 1),
                           waits=(wld + [fr] if kc == 0 else ()))
                mm_tick = c_pe.n
                sgc = ff_sgc if th == 0 else ff_sgc2
                psg = pb[gb][:, :]
                psu = pb[ub][:, :]
                act(sgc[:], psg, AF.Silu,
                    waits=[(c_pe, mm_tick), state["sgfree"][th]])
                state["bfree"][gb] = (c_ac, c_ac.n)
                if not shared:
                    tt(sgc[:], sgc[:], ff_cbc[:, th * 512:(th + 1) * 512],
                       ALU.mult, waits=w(c_ac) + [(c_sy, cbw)])
                    tt(ff_act[:, icp, th * 512:(th + 1) * 512], psu, sgc[:],
                       ALU.mult)
                else:
                    tt(ff_act[:, icp, th * 512:(th + 1) * 512], psu, sgc[:],
                       ALU.mult, waits=w(c_ac))
                state["bfree"][ub] = (c_ve, c_ve.n)
                state["sgfree"][th] = (c_ve, c_ve.n)
                state["seq"] = seq + 1
            state["w13"][ipar] = c_pe.n
        mmr = w(c_ve)
        for jc in range(KC):
            wpar = jc % 2
            wbuf = ff_w2[wpar]
            srcw = sw2t_d[:, jc * 128:(jc + 1) * 128] if shared else \
                w2t_d[eidx * IFF:(eidx + 1) * IFF, jc * 128:(jc + 1) * 128]
            dma("gpsimd", wbuf[:, 0:nih, :],
                srcw.rearrange("(ic p) c -> p ic c", p=128),
                waits=[(c_pe, state["w2"][wpar])])
            wld = w(c_gp)
            for th in range(2):
                ps = pb[4 + wpar][:, :]
                for icp in range(nih):
                    mm(ps, wbuf[:, icp, :],
                       ff_act[:, icp, th * 512:(th + 1) * 512],
                       start=(icp == 0), stop=(icp == nih - 1),
                       waits=(wld + mmr + [state["yfree"][wpar]]
                              if icp == 0 else ()))
                tt(ff_y[:, jc, th * 512:(th + 1) * 512],
                   ff_y[:, jc, th * 512:(th + 1) * 512], ps[:, :], ALU.add,
                   waits=w(c_pe))
                state["yfree"][wpar] = (c_ve, c_ve.n)
            state["w2"][wpar] = c_pe.n
        state["cbc"] = c_ve.n

    for e in range(NEXP + 1):
        expert(e)

    barrier()
    if "yacc" in dbg:
        dma("sync", dbg["yacc"][:], flat(ff_y))
        barrier()

    for oc in range(KC):
        dma("sync", ff_xo[:], s_x1[oc * 128:(oc + 1) * 128, :], waits=w(c_ve))
        tt(ff_xo[:], ff_xo[:], ff_y[:, oc, :], ALU.add, waits=w(c_sy))
        dma("sync", out_d[oc * 128:(oc + 1) * 128, :], ff_xo[:], waits=w(c_ve))
    op("gpsimd", "nop", waits=w(c_sy))
    return _finish(nc, rec, st, dbg)


def _finish(nc, rec, st, dbg):
    with nc.Block() as block:
        @block.sync
        def _(e):
            rec.replay("sync", e)

        @block.tensor
        def _(e):
            rec.replay("tensor", e)

        @block.vector
        def _(e):
            rec.replay("vector", e)

        @block.scalar
        def _(e):
            rec.replay("scalar", e)

        @block.gpsimd
        def _(e):
            rec.replay("gpsimd", e)

    st.close()
    return nc, dbg


def _prep_shared(i):
    f32 = np.float32
    w_in = np.ascontiguousarray(
        np.pad(np.asarray(i["L0_attn_in_proj_w"], f32), [(0, 0), (0, 64)]))
    b_in = np.pad(np.asarray(i["L0_attn_in_proj_b"], f32), (0, 64)) \
        .reshape(67, 128).T.copy()
    cw = np.asarray(i["L0_attn_conv_w"], f32)
    conv_w = cw.reshape(NOC_CONV, 128, 4).transpose(1, 0, 2).reshape(128, -1)
    lt = np.arange(128)
    m0 = (lt[:, None] <= np.arange(256)[None, :]).astype(f32)
    m1 = ((lt[:, None] + 128) <= np.arange(256)[None, :]).astype(f32)
    return {
        "w_in": w_in,
        "b_in": np.ascontiguousarray(b_in),
        "conv_w": np.ascontiguousarray(conv_w),
        "dt_bias": np.asarray(i["L0_attn_dt_bias"], f32).reshape(64, 1),
        "a_log": np.asarray(i["L0_attn_A_log"], f32).reshape(64, 1),
        "d_row": np.asarray(i["L0_attn_D"], f32).reshape(1, 64),
        "nw": np.asarray(i["L0_attn_norm_weight"], f32).reshape(1, INTER),
        "w_out": np.ascontiguousarray(np.asarray(i["L0_attn_out_proj_w"], f32)),
        "b_out": np.asarray(i["L0_attn_out_proj_b"], f32)
            .reshape(16, 128).T.copy(),
        "gate_wt": np.ascontiguousarray(np.asarray(i["L0_ffn_gate_w"], f32).T),
        "w13t": np.ascontiguousarray(
            np.asarray(i["L0_ffn_w13"], f32).transpose(0, 2, 1)
            .reshape(NEXP, HID, 2, 11, 128).transpose(0, 1, 3, 2, 4)
            .reshape(NEXP * HID, 2 * IFF)),
        "w2t": np.ascontiguousarray(np.asarray(i["L0_ffn_w2"], f32)
                                    .transpose(0, 2, 1)).reshape(NEXP * IFF,
                                                                 HID),
        "sw13t": np.ascontiguousarray(
            np.asarray(i["L0_ffn_sw13"], f32).T
            .reshape(HID, 2, 22, 128).transpose(0, 2, 1, 3).reshape(HID, 5632)),
        "sw2t": np.ascontiguousarray(np.asarray(i["L0_ffn_sw2"], f32).T),
        "id_bf": np.eye(128).astype(bfd),
        "id_f": np.eye(128).astype(f32),
        "diag01": np.eye(128).astype(bfd),
        "maskT": np.ascontiguousarray(np.concatenate([m0, m1], axis=1)),
    }


_NC_CACHE = {}


def kernel(**inputs):
    if "nc" not in _NC_CACHE:
        _NC_CACHE["nc"] = build_program()
    nc, _ = _NC_CACHE["nc"]
    shared = _prep_shared(inputs)
    x = np.asarray(inputs["L0_attn_hidden_states"], np.float32)
    in_maps = []
    for c in range(8):
        b, half = c // 2, c % 2
        x_own = np.ascontiguousarray(x[b, half * T:(half + 1) * T, :].T)
        x_prev = np.ascontiguousarray(x[b, 0:T, :].T) if half == 1 else x_own
        m = dict(shared)
        m.update({"x": x_own, "xp": x_prev,
                  "sel": np.full((128, 1), float(half), np.float32)})
        in_maps.append(m)
    res = run_bass_kernel_spmd(nc, in_maps, core_ids=list(range(8)))
    out = np.empty((4, 2048, 2048), np.float32)
    for c in range(8):
        b, half = c // 2, c % 2
        out[b, half * T:(half + 1) * T, :] = np.asarray(res.results[c]["y"]).T
    return out


# version 10 (final)
# speedup vs baseline: 1.0652x; 1.0652x over previous
import contextlib
import numpy as np
import ml_dtypes
import concourse.bass as bass
import concourse.mybir as mybir
from concourse.bass_utils import run_bass_kernel_spmd

F32 = mybir.dt.float32
BF16 = mybir.dt.bfloat16
ALU = mybir.AluOpType
AF = mybir.ActivationFunctionType
AXX = mybir.AxisListType.X
bfd = ml_dtypes.bfloat16

HID = 2048
KC = 16
INTER = 4096
NOC_CONV = 34
H = 64
T = 1024
EPS = 1e-5
NEXP = 16
IFF = 1408


class Ctr:
    def __init__(self, handle):
        self.h = handle
        self.n = 0


class Rec:
    ENGINES = ("sync", "tensor", "vector", "scalar", "gpsimd")

    def __init__(self):
        self.ops = {e: [] for e in self.ENGINES}

    def emit(self, eng, fn):
        self.ops[eng].append(fn)

    def replay(self, eng, h):
        for fn in self.ops[eng]:
            fn(h)


def build_program(debug=(), upto="full"):
    nc = bass.Bass("TRN2", target_bir_lowering=False)
    rec = Rec()
    st = contextlib.ExitStack()

    def din(name, shape, dt=F32):
        return nc.dram_tensor(name, shape, dt, kind="ExternalInput").ap()

    def dscr(name, shape, dt=F32):
        return nc.dram_tensor(name, shape, dt, kind="Internal").ap()

    x_d = din("x", [HID, T])
    xp_d = din("xp", [HID, T])
    sel_d = din("sel", [128, 1])
    w_in_d = din("w_in", [HID, 8576])
    b_in_d = din("b_in", [128, 67])
    conv_w_d = din("conv_w", [128, NOC_CONV * 4])
    dtb_d = din("dt_bias", [64, 1])
    alog_d = din("a_log", [64, 1])
    drow_d = din("d_row", [1, 64])
    nw_d = din("nw", [1, INTER])
    w_out_d = din("w_out", [INTER, HID])
    b_out_d = din("b_out", [128, 16])
    gate_wt_d = din("gate_wt", [HID, 16])
    w13t_d = din("w13t", [NEXP * HID, 2 * IFF])
    w2t_d = din("w2t", [NEXP * IFF, HID])
    sw13t_d = din("sw13t", [HID, 5632])
    sw2t_d = din("sw2t", [2816, HID])
    id_bf_d = din("id_bf", [128, 128], BF16)
    id_f_d = din("id_f", [128, 128])
    diag01_d = din("diag01", [128, 128], BF16)
    maskT_d = din("maskT", [128, 512])

    out_d = nc.dram_tensor("y", [HID, T], F32, kind="ExternalOutput").ap()
    dbg = {}
    for name, shape, dt in [
        ("hs", [2 * T, INTER], BF16), ("gate", [INTER, T], BF16),
        ("x1", [HID, T], F32), ("normed", [INTER, T], BF16),
        ("comb", [NEXP, T], F32), ("S", [128, INTER], F32),
        ("dtA", [128, 2 * T], F32), ("btm", [128, KC * 128], BF16),
        ("ctm", [128, KC * 128], BF16), ("ln", [128, KC * 2 * T], BF16),
        ("x2", [128, KC * T], BF16), ("yacc", [128, KC * T], F32),
    ]:
        if name in debug:
            dbg[name] = nc.dram_tensor("dbg_" + name, shape, dt,
                                       kind="ExternalOutput").ap()

    s_hs = dscr("s_hs", [2 * T, INTER], BF16)
    s_gate = dscr("s_gate", [INTER, T], BF16)
    s_normed = dscr("s_normed", [INTER, T], BF16)
    s_x1 = dscr("s_x1", [HID, T])
    s_acum = dscr("s_acum", [2, 64, 256])
    s_comb = dscr("s_comb", [NEXP, T])

    def sbt(name, shape, dt=F32):
        return st.enter_context(nc.sbuf_tensor(name, shape, dt))

    sel_v = sbt("sel_v", [128, 1])
    ones_col = sbt("ones_col", [128, 1])
    ones_row = sbt("ones_row", [1, 128])
    id_bf = sbt("id_bf_sb", [128, 128], BF16)
    id_f = sbt("id_f_sb", [128, 128])
    diag01 = sbt("diag01_sb", [128, 128], BF16)
    maskT = sbt("maskT_sb", [128, 512])
    negexpA = sbt("negexpA", [64, 1])
    dtb_s = sbt("dtb_s", [64, 1])
    b_in_s = sbt("b_in_s", [128, 67])
    conv_w_s = sbt("conv_w_s", [128, NOC_CONV * 4])
    b_out_s = sbt("b_out_s", [128, 16])
    dbc = sbt("dbc", [128, 64])
    row_s = sbt("row_s", [1, 64])
    cpool = sbt("cpool", [128, 4])

    AFW = 21504
    ABW = 54272
    AF32 = sbt("AF32", [128, AFW])
    ABF = sbt("ABF", [128, ABW], BF16)

    def _view(base, off, p, dims):
        n = int(np.prod(dims))
        a = base[0:p, off:off + n]
        if len(dims) > 1:
            names = " ".join(f"d{i}" for i in range(len(dims)))
            a = a.rearrange(f"p ({names}) -> p {names}",
                            **{f"d{i}": d for i, d in enumerate(dims)})
        return a

    def vf(off, p, *dims):
        return _view(AF32, off, p, dims)

    def vb(off, p, *dims):
        return _view(ABF, off, p, dims)

    dtA = vf(0, 128, 2 * T)
    lndt = vf(2048, 64, 2 * T)
    S_st = vf(4096, 128, INTER)
    Racum = [vf(8192, 128, 2048), vf(10240, 128, 2048)]
    p1_xt = vf(12288, 128, KC, 256)
    p1_xsq = vf(16384, 128, KC, 256)
    p1_st = vf(20480, 1, 512)
    p1_ab = vf(20992, 1, 512)
    p1_dtr = vf(16384, 64, 512)
    p1_sp1 = vf(16896, 64, 512)
    p1_sp2 = vf(17408, 64, 512)
    p2_y0 = vf(12288, 128, INTER)
    p2_expD = [vf(16384, 128, 768), vf(17152, 128, 768)]
    p2_acf = vf(17920, 64, 256)
    p2_vv = vf(18176, 64, 256)
    p2_actm = vf(18432, 128, 128)
    p2_act = vf(18432, 128, 2, 64)
    p2_eatm = vf(18560, 128, 128)
    p2_eat = vf(18560, 128, 2, 64)
    p2_csb = vf(18688, 128, 64)
    p2_ecs = vf(18752, 128, 64)
    p2_decm = vf(18816, 128, 128)
    p2_dec = vf(18816, 128, 2, 64)
    p2_wtmm = vf(18944, 128, 128)
    p2_wtm = vf(18944, 128, 2, 64)
    p2_dttm = vf(19072, 128, 128)
    p2_t1 = vf(19200, 128, 512)
    p2_t2 = vf(19712, 128, 512)
    p2_rst = vf(20224, 128, 2)
    p2_srow = vf(20228, 1, 64)
    p2_vtm = vf(20292, 128, 2, 64)
    p3_at = vf(12288, 128, 512)
    p3_xt = vf(12800, 128, 512)
    ff_y = vf(0, 128, KC, T)
    ff_lnxt = vf(0, 128, KC, 256)
    ff_lnsq = vf(4096, 128, KC, 256)
    ff_lnst = vf(8192, 1, 512)
    ff_lnab = vf(8704, 1, 512)
    ff_cbc = vf(16384, 128, T)
    ff_sgc = vf(17408, 128, 512)
    ff_sgc2 = vf(20032, 128, 512)
    ff_sc = vf(17920, 128, 16)
    ff_sc2 = vf(17936, 128, 16)
    ff_m = vf(17952, 128, 1)
    ff_den = vf(17956, 128, 1)
    ff_cmb = vf(17984, 16, 1024)
    ff_xo = vf(19008, 128, 1024)

    B_tm = vb(0, 128, KC, 128)
    C_tm = vb(2048, 128, KC, 128)
    S_bf = vb(4096, 128, INTER)
    p1_ln = vb(8192, 128, KC, 2 * T)
    p1_ci = [vb(40960, 128, 2056), vb(43016, 128, 2056)]
    p1_dg = [vb(45072, 128, 4, 128), vb(45584, 128, 4, 128)]
    p1_w = [vb(46096, 128, KC, 128), vb(48144, 128, KC, 128)]
    p1_stg = vb(50192, 128, 16, 128)
    p1_stgf = vb(50192, 128, 2048)
    p1_gstg = [vb(40960, 128, 1024), vb(43016, 128, 1024)]
    p2_hsb = [vb(8192, 128, 2, INTER), vb(41856, 128, 2, INTER)]
    p2_gf = vb(16384, 128, 32, 256)
    p2_gt = vb(24576, 128, 2, INTER)
    p2_gm = vb(32768, 128, 512)
    p2_gm2 = vb(41472, 128, 384)
    p2_mt = [vb(33280, 128, 768), vb(34048, 128, 768)]
    p2_nw = vb(34816, 128, INTER)
    p2_u = [vb(38912, 128, 512), vb(39424, 128, 512)]
    p2_nst = [vb(39936, 128, 512), vb(40448, 128, 512)]
    p2_bfm = vb(40960, 128, 256)
    p2_cfm = vb(41216, 128, 256)
    p2_y1 = ABF[0:128, 16384:16384 + 8192].bitcast(F32)
    p3_nm = vb(0, 128, 32, T)
    p3_w = [vb(32768, 128, 32, 128), vb(36864, 128, 32, 128)]
    ff_x2 = vb(0, 128, KC, T)
    ff_act = vb(16384, 128, 22, T)
    ff_w13 = [vb(38912, 128, KC, 256), vb(43008, 128, KC, 256)]
    ff_w2 = [vb(47104, 128, 22, 128), vb(49920, 128, 22, 128)]
    ff_gw = vb(52736, 128, 16, 16)

    pb = [st.enter_context(nc.psum_tensor(f"pb{i}", [128, 512], F32))
          for i in range(8)]

    c_sy = Ctr(st.enter_context(nc.semaphore("c_sy")))
    c_gp = Ctr(st.enter_context(nc.semaphore("c_gp")))
    c_pe = Ctr(st.enter_context(nc.semaphore("c_pe")))
    c_ve = Ctr(st.enter_context(nc.semaphore("c_ve")))
    c_ac = Ctr(st.enter_context(nc.semaphore("c_ac")))
    ALLC = {"sync": c_sy, "gpsimd": c_gp, "tensor": c_pe,
            "vector": c_ve, "scalar": c_ac}

    def op(eng, method, *args, waits=(), inc=None, k=1, **kw):
        if inc is not None:
            inc.n += k
        wlist = tuple((c.h, v) for c, v in waits if v > 0)

        def fn(h, method=method, args=args, kw=kw, wlist=wlist, inc=inc, k=k):
            for hh, v in wlist:
                h.wait_ge(hh, v)
            instr = getattr(h, method)(*args, **kw)
            if inc is not None:
                instr.then_inc(inc.h, k)
        rec.emit(eng, fn)

    def w(*ctrs):
        return [(c, c.n) for c in ctrs]

    def barrier():
        snap = [(c, c.n) for c in ALLC.values()]
        for eng, c in ALLC.items():
            op(eng, "nop", waits=[(cc, v) for cc, v in snap if cc is not c],
               inc=c)

    def dma(eng, out, in_, waits=()):
        ctr = c_sy if eng == "sync" else c_gp
        op(eng, "dma_start", waits=waits, inc=ctr, k=16, out=out, in_=in_)

    def mm(out, lhsT, rhs, start=True, stop=True, waits=()):
        op("tensor", "matmul", out, lhsT, rhs, start=start, stop=stop,
           waits=waits, inc=(c_pe if stop else None))

    def tt(out, a, b, alu, waits=(), eng="vector"):
        op(eng, "tensor_tensor", out, a, b, alu, waits=waits,
           inc=(c_ve if eng == "vector" else c_gp))

    def ts(out, a, s1, alu, waits=(), eng="vector"):
        op(eng, "tensor_scalar", out, a, s1, None, alu, waits=waits,
           inc=(c_ve if eng == "vector" else c_gp))

    def act(out, in_, func, waits=(), **kw):
        op("scalar", "activation", out, in_, func, waits=waits, inc=c_ac, **kw)

    def cp(out, in_, waits=(), eng="vector"):
        op(eng, "tensor_copy", out, in_, waits=waits,
           inc=(c_ve if eng == "vector" else c_gp))

    def red(out, in_, alu, waits=()):
        op("vector", "tensor_reduce", out, in_, AXX, alu, waits=waits, inc=c_ve)

    def bc8(apview, h0):
        return apview[:, h0:h0 + 8].unsqueeze(2).broadcast_to((128, 8, 64))

    def flat(v):
        return v.rearrange("p a b -> p (a b)")

    dma("sync", sel_v[:], sel_d[:])
    dma("sync", id_bf[:], id_bf_d[:])
    dma("sync", id_f[:], id_f_d[:])
    dma("sync", diag01[:], diag01_d[:])
    dma("sync", maskT[:], maskT_d[:])
    dma("sync", dtb_s[:], dtb_d[:])
    dma("sync", b_in_s[:], b_in_d[:])
    dma("sync", conv_w_s[:], conv_w_d[:])
    dma("sync", b_out_s[:], b_out_d[:])
    dma("sync", row_s[:], drow_d[:])
    dma("sync", negexpA[:], alog_d[:])
    op("vector", "memset", ones_col[:], 1.0, inc=c_ve)
    op("vector", "memset", ones_row[:], 1.0, inc=c_ve)
    op("vector", "memset", S_st[:], 0.0, inc=c_ve)
    op("vector", "memset", S_bf[:], 0.0, inc=c_ve)
    act(negexpA[:], negexpA[:], AF.Exp, waits=w(c_sy))
    ts(negexpA[:], negexpA[:], -1.0, ALU.mult, waits=w(c_ac))
    mm(pb[0][:, 0:64], ones_row[:], row_s[:], waits=w(c_sy, c_ve))
    cp(dbc[:], pb[0][:, 0:64], waits=w(c_pe))
    for ci, cv in enumerate((EPS, 1.0, 1e-30)):
        op("vector", "memset", cpool[:, ci:ci + 1], cv, inc=c_ve)
        nc.const_aps.register(F32, cv, cpool[:, ci:ci + 1])

    def ln_quarter(src_slice, dst_fn, xt, xsq, strow, abrow):
        dma("sync", xt[:],
            src_slice.rearrange("(kc p) t -> p kc t", p=128),
            waits=w(c_ve, c_ac, c_pe))
        ld = w(c_sy)
        act(flat(xsq), flat(xt), AF.Square, waits=ld)
        for kc in range(KC):
            mm(pb[7][0:1, 0:256], ones_col[:], xt[:, kc, :],
               start=(kc == 0), stop=(kc == KC - 1),
               waits=(ld + w(c_ve) if kc == 0 else ()))
        for kc in range(KC):
            mm(pb[7][0:1, 256:512], ones_col[:], xsq[:, kc, :],
               start=(kc == 0), stop=(kc == KC - 1),
               waits=(w(c_ac) if kc == 0 else ()))
        ts(strow[:], pb[7][0:1, :], 1.0 / HID, ALU.mult, waits=w(c_pe))
        mrow, e2row = strow[0:1, 0:256], strow[0:1, 256:512]
        arow, brow = abrow[0:1, 0:256], abrow[0:1, 256:512]
        tt(brow, mrow, mrow, ALU.mult, waits=w(c_ve))
        tt(brow, e2row, brow, ALU.subtract, waits=w(c_ve))
        act(brow, brow, AF.Ln, bias=EPS, waits=w(c_ve))
        act(arow, brow, AF.Exp, scale=-0.5)
        tt(brow, mrow, arow, ALU.mult, waits=w(c_ac))
        ts(brow, brow, -1.0, ALU.mult, waits=w(c_ve))
        mm(pb[6][:, :], ones_row[:], abrow[0:1, :], waits=w(c_ve))
        a_bc = pb[6][:, 0:256].unsqueeze(1).broadcast_to((128, KC, 256))
        b_bc = pb[6][:, 256:512].unsqueeze(1).broadcast_to((128, KC, 256))
        tt(xt[:, :, :], xt[:, :, :], a_bc, ALU.mult, waits=w(c_pe))
        tt(dst_fn(None), xt[:, :, :], b_bc, ALU.add)
        return c_ve.n

    ln_tick = {}
    for qg in range(8):
        sti, q = qg // 4, qg % 4
        src = (xp_d if sti == 0 else x_d)[:, q * 256:(q + 1) * 256]
        off = sti * T + q * 256
        tk = ln_quarter(src, lambda kc, off=off: p1_ln[:, :, off:off + 256],
                        p1_xt, p1_xsq, p1_st, p1_ab)
        ln_tick[(sti, q // 2)] = tk
    if "ln" in dbg:
        barrier()
        dma("sync", dbg["ln"][:], flat(p1_ln))
        barrier()

    w_used = [0, 0]

    def load_wtile(buf, col0, par):
        dma("gpsimd", buf[:, :, :],
            w_in_d[:, col0:col0 + 128].rearrange("(kc p) c -> p kc c", p=128),
            waits=[(c_pe, w_used[par])])

    pfree = {}

    def proj_sweep(wbuf, n_m, banks, evac_fn, sti_list=(0, 1)):
        gi = 0
        for sti in sti_list:
            for th in range(2):
                bank = banks[gi % len(banks)]
                ps = bank[0:n_m, :]
                for kc in range(KC):
                    mm(ps, wbuf[:, kc, 0:n_m],
                       p1_ln[:, kc, sti * T + th * 512: sti * T + th * 512 + 512],
                       start=(kc == 0), stop=(kc == KC - 1),
                       waits=((w(c_gp) if gi == 0 else []) +
                              [pfree.get(id(bank), (c_ve, 0)),
                               (c_ve, ln_tick[(sti, th)])]
                              if kc == 0 else ()))
                pfree[id(bank)] = evac_fn(ps, sti, th)
                gi += 1

    for oc in range(NOC_CONV):
        par = oc % 2
        ci = p1_ci[par]
        dg = p1_dg[par]
        load_wtile(p1_w[par], INTER + oc * 128, par)

        def evac(ps, sti, th, ci=ci, oc=oc):
            dst = ci[:, 3 + sti * 1027 + th * 512: 3 + sti * 1027 + th * 512 + 512]
            act(dst, ps, AF.Identity, bias=b_in_s[:, 32 + oc:33 + oc], waits=w(c_pe))
            return (c_ac, c_ac.n)

        proj_sweep(p1_w[par], 128, [pb[par], pb[4 + par]], evac)
        w_used[par] = c_pe.n
        op("vector", "memset", ci[:, 0:3], 0.0, waits=w(c_ac), inc=c_ve)
        ts(ci[:, 1027:1030], ci[:, 1024:1027], sel_v[:], ALU.mult)
        for k in range(4):
            ts(dg[:, k, :], diag01[:], conv_w_s[:, 4 * oc + k:4 * oc + k + 1],
               ALU.mult)
        dvr = w(c_ve)
        for tbq in range(4):
            bank = pb[2 + (tbq % 2)]
            ps = bank[:, :]
            for tb4 in range(4):
                tb = tbq * 4 + tb4
                sti, tbl = tb // 8, tb % 8
                base = sti * 1027 + tbl * 128
                for k in range(4):
                    mm(ps[:, tb4 * 128:tb4 * 128 + 128],
                       ci[:, base + k: base + k + 128], dg[:, k, :],
                       start=(k == 0), stop=(k == 3),
                       waits=((dvr if tb4 == 0 and k == 0 else []) +
                              ([pfree.get(id(bank), (c_ve, 0))]
                               if (tb4 == 0 and k == 0) else [])))
            if oc < 32:
                act(p1_stgf[:, tbq * 512:(tbq + 1) * 512], ps[:, :], AF.Silu,
                    waits=w(c_pe) + (w(c_sy) if tbq == 0 else []))
            else:
                dstp = B_tm if oc == 32 else C_tm
                for tb4 in range(4):
                    tb = tbq * 4 + tb4
                    act(dstp[:, tb, :], ps[:, tb4 * 128:tb4 * 128 + 128],
                        AF.Silu, waits=(w(c_pe) if tb4 == 0 else ()))
            pfree[id(bank)] = (c_ac, c_ac.n)
        if oc < 32:
            dma("sync", s_hs.rearrange("(tb p) c -> p tb c", p=128)
                [:, :, oc * 128:oc * 128 + 128], p1_stg[:], waits=w(c_ac))
        w_used[par] = c_pe.n

    load_wtile(p1_w[0], 8448, 0)

    def dt_evac(ps, sti, th):
        col = sti * T + th * 512
        act(p1_dtr[:], ps, AF.Identity, bias=b_in_s[0:64, 66:67], waits=w(c_pe))
        dt_evac.tick = c_ac.n
        ts(p1_dtr[:], p1_dtr[:], dtb_s[:, 0:1], ALU.add, waits=w(c_ac))
        act(p1_sp1[:], p1_dtr[:], AF.Abs, waits=w(c_ve))
        act(p1_sp1[:], p1_sp1[:], AF.Exp, scale=-1.0)
        act(p1_sp1[:], p1_sp1[:], AF.Ln, bias=1.0)
        ts(p1_sp2[:], p1_dtr[:], 0.0, ALU.max, waits=w(c_ac))
        tt(dtA[0:64, col:col + 512], p1_sp2[:], p1_sp1[:], ALU.add)
        ts(dtA[64:128, col:col + 512], dtA[0:64, col:col + 512],
           negexpA[:], ALU.mult)
        act(lndt[:, col:col + 512], dtA[0:64, col:col + 512], AF.Ln,
            bias=1e-30, waits=w(c_ve))
        return (c_ac, dt_evac.tick)

    proj_sweep(p1_w[0], 64, [pb[4]], dt_evac)
    w_used[0] = c_pe.n
    p1_ac_tick = c_ac.n
    p1_sy_tick = c_sy.n

    for oc in range(32):
        par = oc % 2
        load_wtile(p1_w[par], oc * 128, par)
        stg = p1_gstg[par]

        def gevac(ps, sti, th, stg=stg, oc=oc):
            act(stg[:, th * 512:(th + 1) * 512], ps, AF.Identity,
                bias=b_in_s[:, oc:oc + 1], waits=w(c_pe) + w(c_sy))
            return (c_ac, c_ac.n)

        proj_sweep(p1_w[par], 128, [pb[par], pb[4 + par]], gevac, sti_list=(1,))
        dma("sync", s_gate[oc * 128:(oc + 1) * 128, :], stg[:], waits=w(c_ac))
        w_used[par] = c_pe.n


    barrier()
    for nm, src in [("hs", s_hs), ("gate", s_gate)]:
        if nm in dbg:
            dma("sync", dbg[nm][:], src[:])
    if "dtA" in dbg:
        dma("sync", dbg["dtA"][:], dtA)
    if "btm" in dbg:
        dma("sync", dbg["btm"][:], flat(B_tm))
    if "ctm" in dbg:
        dma("sync", dbg["ctm"][:], flat(C_tm))
    barrier()
    if upto == "p1":
        op("gpsimd", "nop", waits=w(c_sy))
        return _finish(nc, rec, st, dbg)

    dma("gpsimd", p2_nw[:], nw_d[:].partition_broadcast(128).squeeze(1))

    hs_used = {0: (c_pe, 0), 1: (c_pe, 0)}
    hs_used_ve = {}
    hs_used_ac = {}

    def chunk_prelude(cidx, own, extra=()):
        par = cidx % 2
        col = cidx * 256
        rows = cidx * 256
        p2_hs = p2_hsb[par]
        dma("sync", p2_hs[:, 0, :], s_hs[rows:rows + 128, :],
            waits=[hs_used[par], (c_ve, hs_used_ve.get(par, 0)),
                   (c_ac, hs_used_ac.get(par, 0))] + list(extra))
        dma("sync", p2_hs[:, 1, :], s_hs[rows + 128:rows + 256, :])
        if own:
            dma("sync", p2_gf[:, :, :],
                s_gate.rearrange("(cc p) t -> p cc t", p=128)
                [:, :, col - T:col - T + 256],
                waits=w(c_pe, c_ac, c_ve))
        op("vector", "tensor_tensor_scan", p2_acf[:], dtA[64:128, col:col + 256],
           dtA[64:128, col:col + 256], 0.0, ALU.add, ALU.bypass,
           waits=w(c_pe, c_sy), inc=c_ve)
        if own:
            tt(p2_vv[:], lndt[:, col:col + 256], p2_acf[:], ALU.subtract,
               waits=w(c_ve))
            dma("sync", s_acum[par], p2_acf[:], waits=w(c_ve))
            chunk_prelude.acum_tick = c_sy.n
        for lh in range(2):
            mm(pb[2][:, lh * 64:lh * 64 + 64],
               p2_acf[:, lh * 128:lh * 128 + 128], id_f[0:64, 0:64],
               waits=(w(c_ve) if lh == 0 else ()))
            mm(pb[2][:, 128 + lh * 64:128 + lh * 64 + 64],
               dtA[0:64, col + lh * 128:col + lh * 128 + 128], id_f[0:64, 0:64])
        mm(pb[2][0:1, 256:320], p2_acf[:, 255:256], id_f[0:64, 0:64])
        if own:
            for lh in range(2):
                mm(pb[2][:, 384 + lh * 64:384 + lh * 64 + 64],
                   p2_vv[:, lh * 128:lh * 128 + 128], id_f[0:64, 0:64])
        cp(p2_actm[:], pb[2][:, 0:128], waits=w(c_pe))
        cp(p2_dttm[:], pb[2][:, 128:256])
        cp(p2_srow[:], pb[2][0:1, 256:320])
        if own:
            cp(p2_vtm.rearrange("p a b -> p (a b)"), pb[2][:, 384:512])
        mm(pb[2][:, 320:384], ones_row[:], p2_srow[:], waits=w(c_ve))
        cp(p2_csb[:], pb[2][:, 320:384], waits=w(c_pe))
        if own:
            act(p2_eatm[:], p2_actm[:], AF.Exp, waits=w(c_ve))
        act(p2_ecs[:], p2_csb[:], AF.Exp, waits=w(c_ve))
        for lh in range(2):
            tt(p2_dec[:, lh, :], p2_csb[:], p2_act[:, lh, :], ALU.subtract,
               waits=(w(c_ac) if lh == 0 else ()))
        act(p2_decm[:], p2_decm[:], AF.Exp, waits=w(c_ve))
        tt(p2_wtmm[:], p2_decm[:], p2_dttm[:], ALU.mult, waits=w(c_ac))

    def chunk_states(cidx):
        tb0, tb1 = cidx * 2, cidx * 2 + 1
        par = cidx % 2
        p2_hs = p2_hsb[par]
        for lh in range(2):
            for sl in range(8):
                tt(p2_hs[:, lh, sl * 512:(sl + 1) * 512],
                   p2_hs[:, lh, sl * 512:(sl + 1) * 512],
                   bc8(p2_wtm[:, lh, :], sl * 8), ALU.mult,
                   waits=(w(c_pe, c_gp) if (lh == 0 and sl == 0) else ()))
        for sl in range(8):
            ps = pb[7][:, :]
            mm(ps, B_tm[:, tb0, :], p2_hs[:, 0, sl * 512:(sl + 1) * 512],
               start=True, stop=False, waits=w(c_ve))
            mm(ps, B_tm[:, tb1, :], p2_hs[:, 1, sl * 512:(sl + 1) * 512],
               start=False, stop=True)
            tt(S_st[:, sl * 512:(sl + 1) * 512],
               S_st[:, sl * 512:(sl + 1) * 512], bc8(p2_ecs, sl * 8), ALU.mult,
               waits=w(c_pe))
            tt(S_st[:, sl * 512:(sl + 1) * 512],
               S_st[:, sl * 512:(sl + 1) * 512], ps[:, :], ALU.add)
        cp(S_bf[:], S_st[:])
        hs_used[par] = (c_pe, c_pe.n)
        hs_used_ve[par] = c_ve.n

    for cidx in range(4):
        chunk_prelude(cidx, own=False,
                      extra=([(c_ac, p1_ac_tick), (c_sy, p1_sy_tick)]
                             if cidx == 0 else ()))
        chunk_states(cidx)
    ts(S_st[:], S_st[:], sel_v[:], ALU.mult)
    cp(S_bf[:], S_st[:])
    if "S" in dbg:
        barrier()
        dma("sync", dbg["S"][:], S_st[:], waits=w(c_ve))
        barrier()

    barrier()

    for cidx in range(4, 8):
        par = cidx % 2
        ocol = cidx * 256 - T
        p2_hs = p2_hsb[par]
        chunk_prelude(cidx, own=True)
        for lh in range(2):
            for ccq in range(8):
                ps = pb[3 + (ccq % 2)]
                for c4 in range(4):
                    cc = ccq * 4 + c4
                    mm(ps[:, c4 * 128:c4 * 128 + 128],
                       p2_gf[:, cc, lh * 128:lh * 128 + 128], id_bf[:],
                       waits=(w(c_sy, c_ve) if c4 == 0 else ()))
                cp(p2_gt[:, lh, ccq * 512:(ccq + 1) * 512], ps[:, :],
                   waits=w(c_pe))
        mm(pb[2][:, 0:128], B_tm[:, cidx * 2, :], id_bf[:], waits=w(c_ve))
        mm(pb[2][:, 128:256], B_tm[:, cidx * 2 + 1, :], id_bf[:])
        mm(pb[2][:, 256:384], C_tm[:, cidx * 2, :], id_bf[:])
        mm(pb[2][:, 384:512], C_tm[:, cidx * 2 + 1, :], id_bf[:])
        cp(p2_bfm[:], pb[2][:, 0:256], waits=w(c_pe))
        cp(p2_cfm[:], pb[2][:, 256:512])
        for sh in range(2):
            mm(pb[2][:, sh * 256:sh * 256 + 256],
               p2_bfm[:, sh * 128:sh * 128 + 128], p2_cfm[:],
               waits=(w(c_ve) if sh == 0 else ()))
        tt(p2_gm[:], pb[2][:, :], maskT[:], ALU.mult, waits=w(c_pe))
        cp(p2_gm2[:, 0:256], p2_gm[:, 0:256], waits=w(c_ve))
        cp(p2_gm2[:, 256:384], p2_gm[:, 384:512])
        gmr = w(c_ve)

        rc_ready = {}
        rc_used = [0, 0]
        expD_free = [(c_ac, 0), (c_ac, 0)]
        mt_free = [(c_pe, 0), (c_pe, 0)]
        ybank_free = (c_ve, 0)
        for g8 in range(8):
            rpar = g8 % 2
            h0 = g8 * 8
            dma("sync", Racum[rpar][:],
                s_acum[par, h0:h0 + 8, :].rearrange("h l -> (h l)")
                .partition_broadcast(128),
                waits=[(c_ve, rc_used[rpar]),
                       (c_sy, chunk_prelude.acum_tick)])
            rc_ready[g8] = c_sy.n
            for hp in range(4):
                dpar = hp % 2
                for sub in range(2):
                    hh = hp * 2 + sub
                    h = h0 + hh
                    op("vector", "tensor_scalar",
                       p2_expD[dpar][:, sub * 384:sub * 384 + 256],
                       Racum[rpar][:, hh * 256:hh * 256 + 256],
                       p2_vtm[:, 0, h:h + 1], 30.0, ALU.add, ALU.min,
                       waits=([(c_sy, rc_ready[g8])] if hh == 0 else []) +
                       ([expD_free[dpar]] if sub == 0 else []), inc=c_ve)
                    op("vector", "tensor_scalar",
                       p2_expD[dpar][:, sub * 384 + 256:sub * 384 + 384],
                       Racum[rpar][:, hh * 256 + 128:hh * 256 + 256],
                       p2_vtm[:, 1, h:h + 1], 30.0, ALU.add, ALU.min, inc=c_ve)
                rc_used[rpar] = c_ve.n
                act(p2_mt[dpar][:], p2_expD[dpar][:], AF.Exp,
                    waits=w(c_ve) + [mt_free[dpar]])
                expD_free[dpar] = (c_ac, c_ac.n)
                gm2b = p2_gm2[:].unsqueeze(1).broadcast_to((128, 2, 384)) \
                    .rearrange("p a b -> p (a b)") if False else None
                tt(p2_mt[dpar][:, 0:384], p2_mt[dpar][:, 0:384], p2_gm2[:],
                   ALU.mult, waits=w(c_ac))
                tt(p2_mt[dpar][:, 384:768], p2_mt[dpar][:, 384:768], p2_gm2[:],
                   ALU.mult)
                mtw = w(c_ve)
                for sub in range(2):
                    h = h0 + hp * 2 + sub
                    gidx = h % 8
                    mt = p2_mt[dpar][:, sub * 384:(sub + 1) * 384]
                    mm(pb[3][:, gidx * 64:gidx * 64 + 64], mt[:, 0:128],
                       p2_hs[:, 0, h * 64:h * 64 + 64],
                       waits=(mtw if sub == 0 else []) +
                       ([ybank_free] if gidx == 0 else []))
                    mm(pb[4][:, gidx * 64:gidx * 64 + 64], mt[:, 128:256],
                       p2_hs[:, 0, h * 64:h * 64 + 64], start=True, stop=False)
                    mm(pb[4][:, gidx * 64:gidx * 64 + 64], mt[:, 256:384],
                       p2_hs[:, 1, h * 64:h * 64 + 64], start=False, stop=True)
                    mm(pb[5][:, gidx * 64:gidx * 64 + 64], p2_cfm[:, 0:128],
                       S_bf[:, h * 64:h * 64 + 64],
                       waits=(gmr if gidx == 0 else ()))
                    mm(pb[6][:, gidx * 64:gidx * 64 + 64], p2_cfm[:, 128:256],
                       S_bf[:, h * 64:h * 64 + 64])
                mt_free[dpar] = (c_pe, c_pe.n)
            if True:
                gg = g8
                h0g = gg * 8
                for lh, ydb, yob, ybuf in ((0, 3, 5, p2_y0), (1, 4, 6, p2_y1)):
                    tt(p2_u[lh], p2_hs[:, lh, h0g * 64:h0g * 64 + 512],
                       bc8(dbc, h0g), ALU.mult, eng="gpsimd", waits=w(c_pe))
                    tt(p2_t1[:], pb[yob][:, :], bc8(p2_eat[:, lh, :], h0g),
                       ALU.mult, waits=w(c_pe, c_gp))
                    tt(p2_t2[:], p2_t1[:], pb[ydb][:, :], ALU.add)
                    tt(ybuf[:, h0g * 64:h0g * 64 + 512], p2_t2[:], p2_u[lh],
                       ALU.add)
                ybank_free = (c_ve, c_ve.n)
        yev = w(c_ve)
        chunk_states(cidx)
        for lh, ybuf in ((0, p2_y0), (1, p2_y1)):
            gt = p2_gt[:, lh, :]
            act(gt, gt, AF.Silu, waits=yev)
            tt(ybuf[:], ybuf[:], gt, ALU.mult, waits=w(c_ac))
            act(p2_hs[:, 0, :], ybuf[:], AF.Square,
                accum_out=p2_rst[:, lh:lh + 1],
                waits=w(c_ve, c_sy) + [hs_used[par]])
            ts(p2_rst[:, lh:lh + 1], p2_rst[:, lh:lh + 1], 1.0 / INTER,
               ALU.mult, waits=w(c_ac))
            act(p2_rst[:, lh:lh + 1], p2_rst[:, lh:lh + 1], AF.Ln, bias=EPS,
                waits=w(c_ve))
            act(p2_rst[:, lh:lh + 1], p2_rst[:, lh:lh + 1], AF.Exp, scale=-0.5)
            op("vector", "scalar_tensor_tensor", gt, ybuf[:],
               p2_rst[:, lh:lh + 1], p2_nw[:], ALU.mult, ALU.mult,
               waits=w(c_ac, c_gp), inc=c_ve)
            for ccq in range(8):
                ps = pb[3 + (ccq % 2)]
                nst = p2_nst[ccq % 2]
                for c4 in range(4):
                    cc = ccq * 4 + c4
                    mm(ps[:, c4 * 128:c4 * 128 + 128],
                       p2_gt[:, lh, cc * 128:cc * 128 + 128], id_bf[:],
                       waits=(w(c_ve) + [(c_sy, c_sy.n)] if c4 == 0 else ()))
                cp(nst[:], ps[:, :], waits=w(c_pe))
                dma("sync", s_normed.rearrange("(cc p) t -> p cc t", p=128)
                    [:, ccq * 4:(ccq + 1) * 4,
                     ocol + lh * 128: ocol + lh * 128 + 128],
                    nst[:].rearrange("p (a b) -> p a b", a=4), waits=w(c_ve))
        hs_used_ac[par] = c_ac.n

    barrier()
    if "normed" in dbg:
        dma("sync", dbg["normed"][:], s_normed[:])
        barrier()
    if upto == "p2":
        op("gpsimd", "nop", waits=w(c_sy))
        return _finish(nc, rec, st, dbg)

    dma("sync", p3_nm[:, :, :],
        s_normed.rearrange("(kc p) t -> p kc t", p=128))
    wo_used = [0, 0]
    p3free = {}
    p3at = [p3_at, p3_xt]
    p3xt = [vf(13312, 128, 512), vf(13824, 128, 512)]
    for oc in range(KC):
        par = oc % 2
        wbuf = p3_w[par]
        dma("gpsimd", wbuf[:, :, :],
            w_out_d[:, oc * 128:(oc + 1) * 128]
            .rearrange("(kc p) c -> p kc c", p=128),
            waits=[(c_pe, wo_used[par])])
        wld0 = w(c_gp) + ([(c_sy, c_sy.n)] if oc == 0 else [])
        for th in range(2):
            bi = (oc * 2 + th) % 4
            bank = pb[bi]
            ps = bank[:, :]
            for kc in range(32):
                mm(ps, wbuf[:, kc, :], p3_nm[:, kc, th * 512:(th + 1) * 512],
                   start=(kc == 0), stop=(kc == 31),
                   waits=((wld0 if th == 0 else []) +
                          [p3free.get(id(bank), (c_ve, 0))]
                          if kc == 0 else ()))
            ab = p3at[th]
            xb = p3xt[th]
            act(ab[:], ps, AF.Identity, bias=b_out_s[:, oc:oc + 1],
                waits=w(c_pe, c_ve))
            p3free[id(bank)] = (c_ac, c_ac.n)
            dma("sync", xb[:],
                x_d[oc * 128:(oc + 1) * 128, th * 512:(th + 1) * 512],
                waits=w(c_ve))
            tt(xb[:], xb[:], ab[:], ALU.add, waits=w(c_sy, c_ac))
            dma("sync", s_x1[oc * 128:(oc + 1) * 128, th * 512:(th + 1) * 512],
                xb[:], waits=w(c_ve))
        wo_used[par] = c_pe.n
    barrier()
    if "x1" in dbg:
        dma("sync", dbg["x1"][:], s_x1[:])
        barrier()
    if upto == "p3":
        op("gpsimd", "nop", waits=w(c_sy))
        return _finish(nc, rec, st, dbg)

    for q in range(4):
        src = s_x1[:, q * 256:(q + 1) * 256]
        off = q * 256
        ln_quarter(src, lambda kc, off=off: ff_x2[:, :, off:off + 256],
                   ff_lnxt, ff_lnsq, ff_lnst, ff_lnab)
    barrier()
    if "x2" in dbg:
        dma("sync", dbg["x2"][:], flat(ff_x2))
        barrier()

    dma("gpsimd", ff_gw[:, :, :],
        gate_wt_d[:].rearrange("(kc p) e -> p kc e", p=128))
    sc_all = ff_cbc[:, 0:128].rearrange("p (a b) -> p a b", a=8)
    sc2_all = ff_cbc[:, 128:256].rearrange("p (a b) -> p a b", a=8)
    ge_all = ff_cbc[:, 256:384].rearrange("p (a b) -> p a b", a=8)
    m_all = ff_cbc[:, 384:392].unsqueeze(2)
    den_all = ff_cbc[:, 392:400].unsqueeze(2)
    for tb in range(8):
        ps = pb[0]
        for kc in range(KC):
            mm(ps[0:128, tb * 16:tb * 16 + 16],
               ff_x2[:, kc, tb * 128:(tb + 1) * 128],
               ff_gw[:, kc, :], start=(kc == 0), stop=(kc == KC - 1),
               waits=(w(c_gp, c_ve) if (tb == 0 and kc == 0) else ()))
    cp(sc_all, pb[0][:, 0:128].rearrange("p (a b) -> p a b", a=8),
       waits=w(c_pe))
    def mbc():
        return m_all.broadcast_to((128, 8, 16))
    red(m_all.squeeze(2), sc_all.rearrange("p a b -> p (a b)") if False else m_all.squeeze(2), ALU.max) if False else None
    op("vector", "tensor_reduce", m_all, sc_all, AXX, ALU.max,
       waits=w(c_ve), inc=c_ve)
    tt(sc_all, sc_all, mbc(), ALU.subtract, waits=w(c_ve))
    act(ff_cbc[:, 0:128], ff_cbc[:, 0:128], AF.Exp, waits=w(c_ve))
    cp(sc2_all, sc_all, waits=w(c_ac))
    for r in range(5):
        op("vector", "tensor_reduce", m_all, sc2_all, AXX, ALU.max,
           waits=w(c_ve), inc=c_ve)
        tt(ge_all, sc2_all, mbc(), ALU.is_ge, waits=w(c_ve))
        op("vector", "scalar_tensor_tensor",
           ff_cbc[:, 128:256], ff_cbc[:, 256:384], -4.0,
           ff_cbc[:, 128:256], ALU.mult, ALU.add, waits=w(c_ve), inc=c_ve)
    op("vector", "tensor_reduce", m_all, sc2_all, AXX, ALU.max,
       waits=w(c_ve), inc=c_ve)
    tt(ge_all, sc_all, mbc(), ALU.is_ge, waits=w(c_ve))
    tt(ff_cbc[:, 0:128], ff_cbc[:, 0:128], ff_cbc[:, 256:384], ALU.mult,
       waits=w(c_ve))
    op("vector", "tensor_reduce", den_all, sc_all, AXX, ALU.add,
       waits=w(c_ve), inc=c_ve)
    ts(ff_cbc[:, 392:400], ff_cbc[:, 392:400], 1e-20, ALU.add, waits=w(c_ve))
    op("vector", "reciprocal", ff_cbc[:, 392:400], ff_cbc[:, 392:400],
       waits=w(c_ve), inc=c_ve)
    tt(sc_all, sc_all, den_all.broadcast_to((128, 8, 16)), ALU.mult,
       waits=w(c_ve))
    for tb in range(8):
        psc = pb[1] if tb < 4 else pb[2]
        mm(psc[0:16, (tb % 4) * 128:(tb % 4) * 128 + 128],
           ff_cbc[:, tb * 16:tb * 16 + 16], id_f[:], waits=w(c_ve))
        cp(ff_cmb[:, tb * 128:(tb + 1) * 128],
           psc[0:16, (tb % 4) * 128:(tb % 4) * 128 + 128], waits=w(c_pe))
    dma("sync", s_comb[:], ff_cmb[:], waits=w(c_ve))
    if "comb" in dbg:
        dma("sync", dbg["comb"][:], s_comb[:], waits=w(c_sy))
    op("vector", "memset", flat(ff_y), 0.0, waits=w(c_sy, c_pe, c_ac),
       inc=c_ve)
    barrier()
    if upto == "gate":
        op("gpsimd", "nop", waits=w(c_sy))
        return _finish(nc, rec, st, dbg)

    state = {"w13": [0, 0], "w2": [0, 0], "cbc": 0,
             "bfree": [(c_ve, 0)] * 8, "seq": 0,
             "sgfree": [(c_ve, 0), (c_ve, 0)],
             "yfree": [(c_ve, 0), (c_ve, 0)]}

    def expert(eidx):
        shared = (eidx == NEXP)
        nih = 22 if shared else 11
        if not shared:
            dma("sync", ff_cbc[:],
                s_comb[eidx:eidx + 1, :].partition_broadcast(128).squeeze(1),
                waits=[(c_ve, state["cbc"])])
        cbw = c_sy.n
        for icp in range(nih):
            ipar = icp % 2
            if shared:
                srcw = sw13t_d[:, icp * 256:(icp + 1) * 256]
            else:
                srcw = w13t_d[eidx * HID:(eidx + 1) * HID,
                              icp * 256:(icp + 1) * 256]
            dma("gpsimd", ff_w13[ipar][:, :, :],
                srcw.rearrange("(kc p) c -> p kc c", p=128),
                waits=[(c_pe, state["w13"][ipar])])
            wld = w(c_gp)
            wbuf = ff_w13[ipar]
            GB = (0, 1, 2)
            UB = (3, 6, 7)
            for th in range(2):
                seq = state["seq"]
                gb, ub = GB[seq % 3], UB[seq % 3]
                for gu in range(2):
                    bk = gb if gu == 0 else ub
                    ps = pb[bk][:, :]
                    fr = state["bfree"][bk]
                    for kc in range(KC):
                        mm(ps, wbuf[:, kc, gu * 128:gu * 128 + 128],
                           ff_x2[:, kc, th * 512:(th + 1) * 512],
                           start=(kc == 0), stop=(kc == KC - 1),
                           waits=(wld + [fr] if kc == 0 else ()))
                mm_tick = c_pe.n
                sgc = ff_sgc if th == 0 else ff_sgc2
                psg = pb[gb][:, :]
                psu = pb[ub][:, :]
                act(sgc[:], psg, AF.Silu,
                    waits=[(c_pe, mm_tick), state["sgfree"][th]])
                state["bfree"][gb] = (c_ac, c_ac.n)
                if not shared:
                    tt(sgc[:], sgc[:], ff_cbc[:, th * 512:(th + 1) * 512],
                       ALU.mult, waits=w(c_ac) + [(c_sy, cbw)])
                    tt(ff_act[:, icp, th * 512:(th + 1) * 512], psu, sgc[:],
                       ALU.mult)
                else:
                    tt(ff_act[:, icp, th * 512:(th + 1) * 512], psu, sgc[:],
                       ALU.mult, waits=w(c_ac))
                state["bfree"][ub] = (c_ve, c_ve.n)
                state["sgfree"][th] = (c_ve, c_ve.n)
                state["seq"] = seq + 1
            state["w13"][ipar] = c_pe.n
        mmr = w(c_ve)
        for jc in range(KC):
            wpar = jc % 2
            wbuf = ff_w2[wpar]
            srcw = sw2t_d[:, jc * 128:(jc + 1) * 128] if shared else \
                w2t_d[eidx * IFF:(eidx + 1) * IFF, jc * 128:(jc + 1) * 128]
            dma("gpsimd", wbuf[:, 0:nih, :],
                srcw.rearrange("(ic p) c -> p ic c", p=128),
                waits=[(c_pe, state["w2"][wpar])])
            wld = w(c_gp)
            for th in range(2):
                ps = pb[4 + wpar][:, :]
                for icp in range(nih):
                    mm(ps, wbuf[:, icp, :],
                       ff_act[:, icp, th * 512:(th + 1) * 512],
                       start=(icp == 0), stop=(icp == nih - 1),
                       waits=(wld + mmr + [state["yfree"][wpar]]
                              if icp == 0 else ()))
                tt(ff_y[:, jc, th * 512:(th + 1) * 512],
                   ff_y[:, jc, th * 512:(th + 1) * 512], ps[:, :], ALU.add,
                   waits=w(c_pe))
                state["yfree"][wpar] = (c_ve, c_ve.n)
            state["w2"][wpar] = c_pe.n
        state["cbc"] = c_ve.n

    for e in range(NEXP + 1):
        expert(e)

    barrier()
    if "yacc" in dbg:
        dma("sync", dbg["yacc"][:], flat(ff_y))
        barrier()

    for oc in range(KC):
        dma("sync", ff_xo[:], s_x1[oc * 128:(oc + 1) * 128, :], waits=w(c_ve))
        tt(ff_xo[:], ff_xo[:], ff_y[:, oc, :], ALU.add, waits=w(c_sy))
        dma("sync", out_d[oc * 128:(oc + 1) * 128, :], ff_xo[:], waits=w(c_ve))
    op("gpsimd", "nop", waits=w(c_sy))
    return _finish(nc, rec, st, dbg)


def _finish(nc, rec, st, dbg):
    with nc.Block() as block:
        @block.sync
        def _(e):
            rec.replay("sync", e)

        @block.tensor
        def _(e):
            rec.replay("tensor", e)

        @block.vector
        def _(e):
            rec.replay("vector", e)

        @block.scalar
        def _(e):
            rec.replay("scalar", e)

        @block.gpsimd
        def _(e):
            rec.replay("gpsimd", e)

    st.close()
    return nc, dbg


def _prep_shared(i):
    f32 = np.float32
    w_in = np.ascontiguousarray(
        np.pad(np.asarray(i["L0_attn_in_proj_w"], f32), [(0, 0), (0, 64)]))
    b_in = np.pad(np.asarray(i["L0_attn_in_proj_b"], f32), (0, 64)) \
        .reshape(67, 128).T.copy()
    cw = np.asarray(i["L0_attn_conv_w"], f32)
    conv_w = cw.reshape(NOC_CONV, 128, 4).transpose(1, 0, 2).reshape(128, -1)
    lt = np.arange(128)
    m0 = (lt[:, None] <= np.arange(256)[None, :]).astype(f32)
    m1 = ((lt[:, None] + 128) <= np.arange(256)[None, :]).astype(f32)
    return {
        "w_in": w_in,
        "b_in": np.ascontiguousarray(b_in),
        "conv_w": np.ascontiguousarray(conv_w),
        "dt_bias": np.asarray(i["L0_attn_dt_bias"], f32).reshape(64, 1),
        "a_log": np.asarray(i["L0_attn_A_log"], f32).reshape(64, 1),
        "d_row": np.asarray(i["L0_attn_D"], f32).reshape(1, 64),
        "nw": np.asarray(i["L0_attn_norm_weight"], f32).reshape(1, INTER),
        "w_out": np.ascontiguousarray(np.asarray(i["L0_attn_out_proj_w"], f32)),
        "b_out": np.asarray(i["L0_attn_out_proj_b"], f32)
            .reshape(16, 128).T.copy(),
        "gate_wt": np.ascontiguousarray(np.asarray(i["L0_ffn_gate_w"], f32).T),
        "w13t": np.ascontiguousarray(
            np.asarray(i["L0_ffn_w13"], f32).transpose(0, 2, 1)
            .reshape(NEXP, HID, 2, 11, 128).transpose(0, 1, 3, 2, 4)
            .reshape(NEXP * HID, 2 * IFF)),
        "w2t": np.ascontiguousarray(np.asarray(i["L0_ffn_w2"], f32)
                                    .transpose(0, 2, 1)).reshape(NEXP * IFF,
                                                                 HID),
        "sw13t": np.ascontiguousarray(
            np.asarray(i["L0_ffn_sw13"], f32).T
            .reshape(HID, 2, 22, 128).transpose(0, 2, 1, 3).reshape(HID, 5632)),
        "sw2t": np.ascontiguousarray(np.asarray(i["L0_ffn_sw2"], f32).T),
        "id_bf": np.eye(128).astype(bfd),
        "id_f": np.eye(128).astype(f32),
        "diag01": np.eye(128).astype(bfd),
        "maskT": np.ascontiguousarray(np.concatenate([m0, m1], axis=1)),
    }


_NC_CACHE = {}


def kernel(**inputs):
    if "nc" not in _NC_CACHE:
        _NC_CACHE["nc"] = build_program()
    nc, _ = _NC_CACHE["nc"]
    shared = _prep_shared(inputs)
    x = np.asarray(inputs["L0_attn_hidden_states"], np.float32)
    in_maps = []
    for c in range(8):
        b, half = c // 2, c % 2
        x_own = np.ascontiguousarray(x[b, half * T:(half + 1) * T, :].T)
        x_prev = np.ascontiguousarray(x[b, 0:T, :].T) if half == 1 else x_own
        m = dict(shared)
        m.update({"x": x_own, "xp": x_prev,
                  "sel": np.full((128, 1), float(half), np.float32)})
        in_maps.append(m)
    res = run_bass_kernel_spmd(nc, in_maps, core_ids=list(range(8)))
    out = np.empty((4, 2048, 2048), np.float32)
    for c in range(8):
        b, half = c // 2, c % 2
        out[b, half * T:(half + 1) * T, :] = np.asarray(res.results[c]["y"]).T
    return out
